# version 07
# baseline (speedup 1.0000x reference)
from collections import defaultdict
from contextlib import ExitStack

import numpy as np
import ml_dtypes
import concourse.bass as bass
import concourse.mybir as mybir
from concourse.bass_utils import run_bass_kernel_spmd

F32 = mybir.dt.float32
BF16 = mybir.dt.bfloat16
I32 = mybir.dt.int32
AF = mybir.ActivationFunctionType
ALU = mybir.AluOpType
AX = mybir.AxisListType

NC = 8
B = 8
L = 2048
D = 1024
E = 2048
NST = 16
RK = 64
NEXP = 8
CAP = 4096
FF = 4096
T = B * L
TC = 256
NCH = L // TC
BIG = 1.0e7
LN_EPS = 1e-5
BF = ml_dtypes.bfloat16


class Bbuf:
    __slots__ = ("writers", "readers", "name")

    def __init__(self, name=""):
        self.writers = []
        self.readers = []
        self.name = name


class Prog:
    ENG_ATTR = {"sp": "sync", "act": "scalar", "dve": "vector",
                "pool": "gpsimd", "pe": "tensor"}
    ENGINES = ("sp", "act", "dve", "pool", "pe")
    N_DMA_CHANNELS = 6

    def __init__(self, nc, es):
        self.nc = nc
        self.es = es
        self.items = defaultdict(list)
        self.counts = defaultdict(int)
        self.sems = {}
        for e in self.ENGINES:
            self.sems["c_" + e] = es.enter_context(nc.semaphore("c_" + e))
        self._rot = defaultdict(int)
        self._chan_last = {}
        self.bufs = defaultdict(Bbuf)

    def _chan_sem(self, name):
        if name not in self.sems:
            self.sems[name] = self.es.enter_context(self.nc.semaphore(name))
        return name

    def emit(self, engine, fn, reads=(), writes=(), deps=(), dma=False, chan=None,
             label="", nowait_writes=()):
        reads = [self.bufs[r] if isinstance(r, str) else r for r in reads]
        writes = [self.bufs[w] if isinstance(w, str) else w for w in writes]
        nowait_writes = [self.bufs[w] if isinstance(w, str) else w
                         for w in nowait_writes]
        dep_toks = list(deps)
        for b in reads:
            dep_toks += b.writers
        for b in writes:
            dep_toks += b.writers + b.readers
        if dma:
            if chan is None:
                chan = f"{engine}{self._rot[engine] % self.N_DMA_CHANNELS}"
                self._rot[engine] += 1
            sem = self._chan_sem("q_" + chan)
            step = 16
            prev = self._chan_last.get(sem)
            if prev is not None:
                dep_toks.append(prev)
        else:
            sem, step = "c_" + engine, 1
        self.counts[sem] += step
        tok = (sem, self.counts[sem])
        self.items[engine].append((dep_toks, fn, sem, step, label))
        if dma:
            self._chan_last[sem] = tok
        for b in reads:
            b.readers.append(tok)
        for b in writes:
            b.writers = [tok]
            b.readers = []
        for b in nowait_writes:
            b.writers = [tok]
            b.readers = []
        return tok

    def mm_acc(self, calls, ps_buf, label=""):
        n = len(calls)
        tok = None
        for i, (fn, reads) in enumerate(calls):
            if n == 1:
                tok = self.emit("pe", fn, reads=reads, writes=[ps_buf], label=label)
            elif i == 0:
                tok = self.emit("pe", fn, reads=reads, writes=[ps_buf], label=label)
            elif i == n - 1:
                tok = self.emit("pe", fn, reads=reads, nowait_writes=[ps_buf],
                                label=label)
            else:
                tok = self.emit("pe", fn, reads=reads, label=label)
        return tok

    def build(self, block):
        waited = defaultdict(int)

        def make_section(engine):
            items = self.items[engine]

            def section(eng):
                for dep_toks, fn, sem, step, label in items:
                    need = {}
                    for (dsem, dval) in dep_toks:
                        if waited[(engine, dsem)] >= dval:
                            continue
                        if need.get(dsem, 0) < dval:
                            need[dsem] = dval
                    for dsem, dval in need.items():
                        eng.wait_ge(self.sems[dsem], dval)
                        waited[(engine, dsem)] = dval
                    inst = fn(eng)
                    inst.then_inc(self.sems[sem], step)

            return section

        for engine in self.ENGINES:
            if not self.items[engine]:
                continue
            getattr(block, self.ENG_ATTR[engine])(make_section(engine))


def _consts_np():
    onesm = np.ones((128, 128), np.float32)
    ibf = np.eye(128, dtype=BF)
    identf = np.eye(128, dtype=np.float32)
    eall = np.zeros((16, 16 * 128), np.float32)
    for n in range(16):
        eall[n, n * 128:(n + 1) * 128] = 1.0
    return onesm, identf, ibf, eall


def build_l1():
    nc = bass.Bass("TRN2", target_bir_lowering=False, debug=False)
    xT = nc.dram_tensor("xT", [8, 128, L], F32, kind="ExternalInput").ap()
    winT = nc.dram_tensor("winT", [32, 128, 8, 128], F32, kind="ExternalInput").ap()
    woutT = nc.dram_tensor("woutT", [16, 128, D], F32, kind="ExternalInput").ap()
    xpT = nc.dram_tensor("xpT", [16, 128, 112], F32, kind="ExternalInput").ap()
    dtT = nc.dram_tensor("dtT", [64, E], F32, kind="ExternalInput").ap()
    wg = nc.dram_tensor("wg", [8, 128, 8], F32, kind="ExternalInput").ap()
    eall_i = nc.dram_tensor("eall", [16, 16 * 128], F32, kind="ExternalInput").ap()
    acol = nc.dram_tensor("acol", [128, 256], F32, kind="ExternalInput").ap()
    dcol = nc.dram_tensor("dcol", [128, 16], F32, kind="ExternalInput").ap()
    cbcol = nc.dram_tensor("cbcol", [128, 16], F32, kind="ExternalInput").ap()
    dtbcol = nc.dram_tensor("dtbcol", [128, 16], F32, kind="ExternalInput").ap()
    cwcol = nc.dram_tensor("cwcol", [128, 64], F32, kind="ExternalInput").ap()
    onesm_i = nc.dram_tensor("onesm", [128, 128], F32, kind="ExternalInput").ap()
    ibf_i = nc.dram_tensor("ibf", [128, 128], BF16, kind="ExternalInput").ap()

    lgT_o = nc.dram_tensor("lgT_o", [8, L], F32, kind="ExternalOutput").ap()
    xnT_o = nc.dram_tensor("xnT_o", [8, 128, L], BF16, kind="ExternalOutput").ap()
    s_o = nc.dram_tensor("s_o", [L, D], F32, kind="ExternalOutput").ap()
    winB = nc.dram_tensor("winB", [32, 128, 8, 128], BF16).ap()
    woutB = nc.dram_tensor("woutB", [16, 128, D], BF16).ap()

    es = ExitStack()
    sb = {}
    for name, shape, dt in [
        ("arena", [128, 12288], F32),
        ("mrow", [1, 256], F32), ("erow", [1, 256], F32), ("rrow", [1, 256], F32),
        ("xnc0", [128, 8, TC], BF16), ("xnc1", [128, 8, TC], BF16),
        ("xmc0", [128, 8, TC], BF16), ("xmc1", [128, 8, TC], BF16),
        ("lgT", [8, 512], F32),
        ("wgs", [128, 64], F32),
        ("onesm", [128, 128], F32),
        ("eall", [16, 16 * 128], F32),
        ("ibf", [128, 128], BF16),
        ("acol", [128, 256], F32), ("aneg", [128, 256], F32),
        ("dcol", [128, 16], F32),
        ("cbcol", [128, 16], F32),
        ("dtbcol", [128, 16], F32),
        ("cwcol", [128, 64], F32),
        ("xpTs", [128, 16 * 112], BF16),
        ("dtTs", [64, E], BF16),
        ("winx0", [128, 8, 128], BF16), ("winx1", [128, 8, 128], BF16),
        ("winz0", [128, 8, 128], BF16), ("winz1", [128, 8, 128], BF16),
        ("wout0", [128, D], BF16), ("wout1", [128, D], BF16),
        ("xsl", [128, 16, TC + 3], F32),
        ("xcv", [128, 16, TC], F32),
        ("xcb0", [128, TC], BF16), ("xcb1", [128, TC], BF16),
        ("cvt0", [128, TC], F32), ("cvt1", [128, TC], F32),
        ("dtin", [64, TC], BF16),
        ("bcsb", [16, 512], F32),
        ("spr0", [128, TC], F32), ("spt0", [128, TC], F32),
        ("spr1", [128, TC], F32), ("spt1", [128, TC], F32),
        ("du0", [128, TC], F32), ("du1", [128, TC], F32),
        ("dlt0", [128, TC], F32), ("dlt1", [128, TC], F32),
        ("a0", [128, TC], F32), ("a1", [128, TC], F32),
        ("hall", [128, 16, TC], F32),
        ("hbf", [128, 16, TC], BF16),
        ("carry", [128, 256], F32),
        ("y1a", [128, TC], F32), ("y1b", [128, TC], F32),
        ("zs", [128, 16, TC], F32),
        ("yfs", [128, 16, TC], BF16),
        ("sevac", [128, D], F32),
    ]:
        sb[name] = es.enter_context(nc.sbuf_tensor("s_" + name, shape, dt))

    ar = sb["arena"]
    xt_v = [ar[:, 0:2048].rearrange("p (k t) -> p k t", k=8),
            ar[:, 2048:4096].rearrange("p (k t) -> p k t", k=8)]
    x2_v = ar[:, 4096:6144].rearrange("p (k t) -> p k t", k=8)
    xn32_v = ar[:, 6144:8192].rearrange("p (k t) -> p k t", k=8)
    bcall_q = ar[:, 0:8192].rearrange("p (n q) -> p n q", n=16)
    bball_v = bcall_q[:, :, 0:256]
    cball_v = bcall_q[:, :, 256:512]
    call_v = ar[:, 8192:12288].rearrange("p (n t) -> p n t", n=16)

    ps = [es.enter_context(nc.psum_tensor(f"ps{i}", [128, 512], F32))
          for i in range(8)]
    PB = [f"bank{i}" for i in range(8)]

    p = Prog(nc, es)

    def load(name, dst_ap, src_ap, cast=False, chan=None, wbuf=None):
        eng = "pool" if cast else "sp"
        return p.emit(eng, lambda eng_, d=dst_ap, s=src_ap: eng_.dma_start(out=d, in_=s),
                      writes=[wbuf or name], dma=True, chan=chan, label="ld_" + name)

    load("onesm", sb["onesm"][:], onesm_i)
    load("eall", sb["eall"][:], eall_i)
    load("ibf", sb["ibf"][:], ibf_i)
    load("acol", sb["acol"][:], acol)
    load("dcol", sb["dcol"][:], dcol)
    load("cbcol", sb["cbcol"][:], cbcol)
    load("dtbcol", sb["dtbcol"][:], dtbcol)
    load("cwcol", sb["cwcol"][:], cwcol)
    load("wgs", sb["wgs"][:].rearrange("p (k e) -> p k e", k=8),
         wg.rearrange("k p e -> p k e"))
    load("xpTs", sb["xpTs"][:].rearrange("p (t r) -> p t r", t=16),
         xpT.rearrange("t p r -> p t r"), cast=True)
    load("dtTs", sb["dtTs"][:], dtT, cast=True)
    p.emit("act", lambda eng: eng.activation(out=sb["aneg"][:], in_=sb["acol"][:],
                                             func=AF.Exp),
           reads=["acol"], writes=["aneg"], label="aexp0")
    p.emit("act", lambda eng: eng.activation(out=sb["aneg"][:], in_=sb["aneg"][:],
                                             func=AF.Copy, scale=-1.0),
           reads=["aneg"], writes=["aneg"], label="aneg")

    xT_p = xT.rearrange("k p t -> p k t")
    for cc in range(8):
        t0 = cc * 256
        xb = f"xt{cc % 2}"
        xv = xt_v[cc % 2]
        load(xb, xv, xT_p[:, :, t0:t0 + 256], chan="xt")
        p.emit("act", lambda eng, xv=xv, xb=xb: eng.activation(
            out=x2_v, in_=xv, func=AF.Square),
            reads=[xb], writes=["x2"], label="sq")
        p.mm_acc([(lambda eng, xv=xv, kt=kt: eng.matmul(
            ps[0][0:1, 0:256], sb["onesm"][:, 0:1], xv[:, kt, :],
            start=(kt == 0), stop=(kt == 7)), [xb, "onesm"]) for kt in range(8)],
            "p0a", label="sxmm")
        p.mm_acc([(lambda eng, kt=kt: eng.matmul(
            ps[1][0:1, 0:256], sb["onesm"][:, 0:1], x2_v[:, kt, :],
            start=(kt == 0), stop=(kt == 7)), ["x2", "onesm"]) for kt in range(8)],
            "p1z", label="sx2mm")
        p.emit("act", lambda eng: eng.activation(
            out=sb["mrow"][0:1, :], in_=ps[0][0:1, 0:256], func=AF.Copy, scale=1.0 / D),
            reads=["p0a"], writes=["mrow"], label="ev_mean")
        p.emit("act", lambda eng: eng.activation(
            out=sb["erow"][0:1, :], in_=ps[1][0:1, 0:256], func=AF.Copy, scale=1.0 / D),
            reads=["p1z"], writes=["erow"], label="ev_ex2")
        p.emit("dve", lambda eng: eng.tensor_mul(sb["rrow"][:], sb["mrow"][:], sb["mrow"][:]),
               reads=["mrow"], writes=["rrow"], label="m2")
        p.emit("dve", lambda eng: eng.tensor_sub(sb["rrow"][:], sb["erow"][:], sb["rrow"][:]),
               reads=["erow", "rrow"], writes=["rrow"], label="var")
        p.emit("dve", lambda eng: eng.tensor_scalar_add(
            sb["rrow"][:], sb["rrow"][:], LN_EPS),
            reads=["rrow"], writes=["rrow"], label="vareps")
        p.emit("act", lambda eng: eng.activation(
            out=sb["rrow"][:], in_=sb["rrow"][:], func=AF.Ln),
            reads=["rrow"], writes=["rrow"], label="lnvar")
        p.emit("act", lambda eng: eng.activation(
            out=sb["rrow"][:], in_=sb["rrow"][:], func=AF.Exp, scale=-0.5),
            reads=["rrow"], writes=["rrow"], label="rstd")
        p.emit("pe", lambda eng: eng.matmul(
            ps[2][:, 0:256], sb["onesm"][0:1, :], sb["mrow"][0:1, :],
            start=True, stop=True),
            reads=["mrow", "onesm"], writes=["p2a"], label="meanbc")
        p.emit("pe", lambda eng: eng.matmul(
            ps[3][:, 0:256], sb["onesm"][0:1, :], sb["rrow"][0:1, :],
            start=True, stop=True),
            reads=["rrow", "onesm"], writes=["p3a"], label="rstdbc")
        for kt in range(8):
            p.emit("dve", lambda eng, xv=xv, kt=kt: eng.tensor_sub(
                xn32_v[:, kt, :], xv[:, kt, :], ps[2][:, 0:256]),
                reads=[xb, "p2a"], writes=[f"xn32_{kt}"], label="xsub")
            p.emit("dve", lambda eng, kt=kt: eng.tensor_mul(
                xn32_v[:, kt, :], xn32_v[:, kt, :], ps[3][:, 0:256]),
                reads=[f"xn32_{kt}", "p3a"], writes=[f"xn32_{kt}"], label="xmul")
            p.emit("act", lambda eng, kt=kt, cc=cc: eng.activation(
                out=sb[f"xnc{cc % 2}"][:, kt, :], in_=xn32_v[:, kt, :],
                func=AF.Copy),
                reads=[f"xn32_{kt}"], writes=[f"xnbf{kt}"], label="xnbf")
        p.mm_acc([(lambda eng, kt=kt: eng.matmul(
            ps[4][0:8, 0:256], sb["wgs"][:, kt * 8:(kt + 1) * 8], xn32_v[:, kt, :],
            start=(kt == 0), stop=(kt == 7)),
            [f"xn32_{kt}", "wgs"]) for kt in range(8)], "p4a", label="lgmm")
        p.emit("act", lambda eng, cc=cc: eng.activation(
            out=sb["lgT"][:, (cc % 2) * 256:(cc % 2) * 256 + 256],
            in_=ps[4][0:8, 0:256], func=AF.Copy),
            reads=["p4a"], writes=[f"lgT{cc}"], label="ev_lg")
        p.emit("sp", lambda eng, cc=cc, t0=t0: eng.dma_start(
            out=lgT_o[:, t0:t0 + 256],
            in_=sb["lgT"][:, (cc % 2) * 256:(cc % 2) * 256 + 256]),
            reads=[f"lgT{cc}"], dma=True, chan="st_lg", label="st_lg")
        p.emit("sp", lambda eng, cc=cc, t0=t0: eng.dma_start(
            out=xnT_o[:, :, t0:t0 + 256].rearrange("k p t -> p k t"),
            in_=sb[f"xnc{cc % 2}"][:]),
            reads=[f"xnbf{kt}" for kt in range(8)], writes=[f"xnT{cc}"],
            dma=True, chan="st_xn", label="st_xn")

    for m in range(32):
        p.emit("pool", lambda eng, m=m: eng.dma_start(
            out=winB[m], in_=winT[m]),
            writes=[f"wpcx{m}"], dma=True, chan=f"pc{m % 4}", label="pcw")
    for k in range(16):
        p.emit("pool", lambda eng, k=k: eng.dma_start(
            out=woutB[k], in_=woutT[k]),
            writes=[f"wpco{k}"], dma=True, chan=f"pc{k % 4}", label="pco")

    xsl_m = [f"xsl_m{m}" for m in range(16)]
    arena_ln = ["xt0", "xt1", "x2"] + [f"xn32_{kt}" for kt in range(8)]
    p.emit("dve", lambda eng: eng.memset(sb["carry"][:], 0.0), writes=["carry"])
    p.emit("dve", lambda eng: eng.memset(sb["xsl"][:], 0.0),
           writes=["xsl_h"] + xsl_m, label="xsl0")

    def ld_xn(c):
        wb = f"xmc{c % 2}"
        p.emit("sp", lambda eng, c=c, wb=wb: eng.dma_start(
            out=sb[wb][:],
            in_=xnT_o[:, :, c * TC:(c + 1) * TC].rearrange("k p t -> p k t")),
            reads=[f"xnT{c}"], writes=[f"xm{c % 2}"], dma=True,
            chan="xl" + str(c % 2), label="ld_xn")

    def ld_winx(c, et):
        wb = f"winx{et % 2}"
        p.emit("sp", lambda eng, et=et, wb=wb: eng.dma_start(
            out=sb[wb][:], in_=winB[et]),
            reads=[f"wpcx{et}"], writes=[wb], dma=True, chan="wx" + str(et % 2),
            label="ld_wx")

    def ld_winz(c, et):
        wb = f"winz{et % 2}"
        p.emit("sp", lambda eng, et=et, wb=wb: eng.dma_start(
            out=sb[wb][:], in_=winB[16 + et]),
            reads=[f"wpcx{16 + et}"], writes=[wb], dma=True,
            chan="wz" + str(et % 2), label="ld_wz")

    def inx(c, et):
        xm = f"xmc{c % 2}"
        wb = f"winx{et % 2}"
        rb, bk = ("p2a", 2) if et % 2 == 0 else ("p4a", 4)
        p.mm_acc([(lambda eng, kt=kt, bk=bk, wb=wb, xm=xm: eng.matmul(
            ps[bk][:, 0:TC], sb[wb][:, kt, :], sb[xm][:, kt, :],
            start=(kt == 0), stop=(kt == 7)),
            [wb, f"xm{c % 2}"]) for kt in range(8)], rb, label="inx")
        p.emit("act", lambda eng, et=et, bk=bk: eng.activation(
            out=sb["xsl"][:, et, 3:3 + TC], in_=ps[bk][:, 0:TC], func=AF.Copy),
            reads=[rb, "xsl_h"], writes=[xsl_m[et]], label="ev_x")

    def inz(c, k):
        xm = f"xmc{c % 2}"
        wb = f"winz{k % 2}"
        p.mm_acc([(lambda eng, kt=kt, wb=wb, xm=xm: eng.matmul(
            ps[1][:, 0:TC], sb[wb][:, kt, :], sb[xm][:, kt, :],
            start=(kt == 0), stop=(kt == 7)),
            [wb, f"xm{c % 2}"]) for kt in range(8)], "p1z", label="inz")

    def conv_piece(c, k):
        p.emit("act", lambda eng, k=k: eng.activation(
            out=sb["cvt0"][:], in_=sb["xsl"][:, k, 0:TC], func=AF.Copy,
            scale=sb["cwcol"][:, k * 4:k * 4 + 1]),
            reads=[xsl_m[k], "xsl_h", "cwcol"], writes=["cvt0"], label="cv0")
        p.emit("dve", lambda eng, k=k: eng.scalar_tensor_tensor(
            out=sb["cvt1"][:], in0=sb["xsl"][:, k, 1:1 + TC],
            scalar=sb["cwcol"][:, k * 4 + 1:k * 4 + 2],
            in1=sb["cvt0"][:], op0=ALU.mult, op1=ALU.add),
            reads=[xsl_m[k], "xsl_h", "cwcol", "cvt0"], writes=["cvt1"],
            label="cv1")
        p.emit("dve", lambda eng, k=k: eng.scalar_tensor_tensor(
            out=sb["cvt0"][:], in0=sb["xsl"][:, k, 2:2 + TC],
            scalar=sb["cwcol"][:, k * 4 + 2:k * 4 + 3],
            in1=sb["cvt1"][:], op0=ALU.mult, op1=ALU.add),
            reads=[xsl_m[k], "xsl_h", "cwcol", "cvt1"], writes=["cvt0"],
            label="cv2")
        p.emit("dve", lambda eng, k=k: eng.scalar_tensor_tensor(
            out=sb["xcv"][:, k, :], in0=sb["xsl"][:, k, 3:3 + TC],
            scalar=sb["cwcol"][:, k * 4 + 3:k * 4 + 4],
            in1=sb["cvt0"][:], op0=ALU.mult, op1=ALU.add),
            reads=[xsl_m[k], "cwcol", "cvt0"], writes=[f"xcvr{k}"], label="cv3")

    def silupair(c, k):
        p.emit("act", lambda eng, k=k: eng.activation(
            out=sb["zs"][:, k, :], in_=ps[1][:, 0:TC], func=AF.Silu),
            reads=["p1z"], writes=[f"zs{k}"], label="ev_z")
        p.emit("act", lambda eng, k=k: eng.activation(
            out=sb["xcv"][:, k, :], in_=sb["xcv"][:, k, :], func=AF.Silu,
            bias=sb["cbcol"][:, k:k + 1]),
            reads=[f"xcvr{k}", "cbcol"], writes=[f"xcv{k}"], label="silu_cv")

    def xcbxp(c, k):
        xcb = f"xcb{k % 2}"
        p.emit("act", lambda eng, k=k, xcb=xcb: eng.activation(
            out=sb[xcb][:], in_=sb["xcv"][:, k, :], func=AF.Copy),
            reads=[f"xcv{k}"], writes=[xcb], label="xcb")
        for rb, bk, c0, c1 in (("p3a", 3, 0, 64), ("p3b", 6, 64, 112)):
            kw = {}
            if k == 0:
                kw["writes"] = [rb]
            elif k == 15:
                kw["nowait_writes"] = [rb]
            p.emit("pe", lambda eng, k=k, bk=bk, c0=c0, c1=c1, xcb=xcb:
                   eng.matmul(
                       ps[bk][0:c1 - c0, 0:TC],
                       sb["xpTs"][:, k * 112 + c0:k * 112 + c1],
                       sb[xcb][:], start=(k == 0), stop=(k == 15)),
                   reads=[xcb, "xpTs"], label="xp", **kw)

    def xp_evacs(c):
        p.emit("act", lambda eng: eng.activation(
            out=sb["dtin"][:], in_=ps[3][0:64, 0:TC], func=AF.Copy),
            reads=["p3a"], writes=["dtin"], label="ev_dtin")
        p.emit("act", lambda eng: eng.activation(
            out=sb["bcsb"][0:16, 0:256], in_=ps[6][0:16, 0:TC], func=AF.Copy),
            reads=["p3b"], writes=["bcsb_b"], label="ev_B")
        p.emit("act", lambda eng: eng.activation(
            out=sb["bcsb"][0:16, 256:512], in_=ps[6][32:48, 0:TC], func=AF.Copy),
            reads=["p3b"], writes=["bcsb_c"], label="ev_C")

    def sel(c):
        for n in range(16):
            bank, bufs = (2, ["p2a", "p2b"]) if n % 2 == 0 else (4, ["p4a", "p4b"])
            p.emit("pe", lambda eng, n=n, bank=bank: eng.matmul(
                ps[bank][:, 0:512], sb["eall"][:, n * 128:(n + 1) * 128],
                sb["bcsb"][0:16, :], start=True, stop=True),
                reads=["bcsb_b", "bcsb_c", "eall"], writes=bufs, label="selmm")
            wl = [f"bball{n}", f"cball{n}"] + (arena_ln if c == 0 else [])
            p.emit("act", lambda eng, n=n, bank=bank: eng.activation(
                out=ar[:, n * 512:(n + 1) * 512], in_=ps[bank][:, 0:512],
                func=AF.Copy),
                reads=bufs, writes=wl, label="ev_bc")

    def emit_dtmm(c, et):
        c0 = (et % 2) * 256
        pbank = f"p0{'ab'[et % 2]}"
        p.emit("pe", lambda eng, et=et, c0=c0: eng.matmul(
            ps[0][:, c0:c0 + TC], sb["dtTs"][:, et * 128:(et + 1) * 128],
            sb["dtin"][:], start=True, stop=True),
            reads=["dtin", "dtTs"], writes=[pbank], label="dtmm")

    def sp_stage(c, et):
        par = et % 2
        c0 = par * 256
        pbank = f"p0{'ab'[par]}"
        sprb, sptb = f"spr{par}", f"spt{par}"
        p.emit("act", lambda eng, et=et, c0=c0, sprb=sprb: eng.activation(
            out=sb[sprb][:], in_=ps[0][:, c0:c0 + TC], func=AF.Relu,
            bias=sb["dtbcol"][:, et:et + 1]),
            reads=[pbank, "dtbcol"], writes=[sprb], label="sp_r")
        p.emit("act", lambda eng, et=et, c0=c0, sptb=sptb: eng.activation(
            out=sb[sptb][:], in_=ps[0][:, c0:c0 + TC], func=AF.Abs,
            bias=sb["dtbcol"][:, et:et + 1]),
            reads=[pbank, "dtbcol"], writes=[sptb], label="sp_a")
        p.emit("act", lambda eng, sptb=sptb: eng.activation(
            out=sb[sptb][:], in_=sb[sptb][:], func=AF.Exp, scale=-1.0),
            reads=[sptb], writes=[sptb], label="sp_e")
        p.emit("act", lambda eng, sptb=sptb: eng.activation(
            out=sb[sptb][:], in_=sb[sptb][:], func=AF.Ln, bias=1.0),
            reads=[sptb], writes=[sptb], label="sp_l")

    def pool_stage(c, et):
        par = et % 2
        sprb, sptb = f"spr{par}", f"spt{par}"
        dltb, dub = f"dlt{par}", f"du{par}"
        p.emit("pool", lambda eng, dltb=dltb, sptb=sptb, sprb=sprb: eng.tensor_tensor(
            out=sb[dltb][:], in0=sb[sptb][:], in1=sb[sprb][:], op=ALU.add),
            reads=[sptb, sprb], writes=[dltb], label="sp_add")
        p.emit("pool", lambda eng, et=et, dltb=dltb, dub=dub: eng.tensor_tensor(
            out=sb[dub][:], in0=sb[dltb][:], in1=sb["xcv"][:, et, :], op=ALU.mult),
            reads=[dltb, f"xcv{et}"], writes=[dub], label="du")

    def cmul_stage(c, et):
        dub = f"du{et % 2}"
        for half in range(2):
            n0 = half * 8
            p.emit("pool", lambda eng, dub=dub, n0=n0: eng.tensor_tensor(
                out=call_v[:, n0:n0 + 8, :],
                in0=sb[dub][:].unsqueeze(1).broadcast_to([128, 8, TC]),
                in1=bball_v[:, n0:n0 + 8, :], op=ALU.mult),
                reads=[dub] + [f"bball{n}" for n in range(n0, n0 + 8)],
                writes=[f"call{half}"], label="cmul")

    def aexp(c, et, n):
        ab = f"a{n % 2}"
        dltb = f"dlt{et % 2}"
        p.emit("act", lambda eng, et=et, n=n, ab=ab, dltb=dltb: eng.activation(
            out=sb[ab][:], in_=sb[dltb][:], func=AF.Exp,
            scale=sb["aneg"][:, et * 16 + n:et * 16 + n + 1]),
            reads=[dltb, "aneg"], writes=[ab], label="aexp")

    def scan(c, et, n):
        ab = f"a{n % 2}"
        p.emit("dve", lambda eng, et=et, n=n, ab=ab: eng.tensor_tensor_scan(
            out=sb["hall"][:, n, :], data0=sb[ab][:],
            data1=call_v[:, n, :],
            initial=sb["carry"][:, et * 16 + n:et * 16 + n + 1],
            op0=ALU.mult, op1=ALU.add),
            reads=[ab, f"call{n // 8}", "carry"], writes=[f"hall{n}"],
            label="scan")

    def hmul(c, et, half):
        n0 = half * 8
        hhalf = [f"hall{n}" for n in range(n0, n0 + 8)]
        cb_h = [f"cball{n}" for n in range(n0, n0 + 8)]
        p.emit("dve", lambda eng, n0=n0: eng.tensor_tensor(
            out=sb["hbf"][:, n0:n0 + 8, :], in0=sb["hall"][:, n0:n0 + 8, :],
            in1=cball_v[:, n0:n0 + 8, :], op=ALU.mult),
            reads=hhalf + cb_h, writes=[f"hbf{half}"], label="hmul")

    def ymm_y1(c, et):
        hall_all = [f"hall{n}" for n in range(16)]
        p.emit("act", lambda eng, et=et: eng.activation(
            out=sb["carry"][:, et * 16:(et + 1) * 16],
            in_=sb["hall"][:, :, TC - 1:TC].rearrange("p a b -> p (a b)"),
            func=AF.Copy),
            reads=hall_all, writes=["carry"], label="carryw")
        p.mm_acc([(lambda eng, n=n: eng.matmul(
            ps[7][:, 0:TC], sb["ibf"][:], sb["hbf"][:, n, :],
            start=(n == 0), stop=(n == 15)), [f"hbf{n // 8}", "ibf"])
            for n in range(16)], PB[7], label="ymm")
        y1b_ = f"y1{'ab'[et % 2]}"
        p.emit("dve", lambda eng, et=et, y1b_=y1b_: eng.scalar_tensor_tensor(
            out=sb[y1b_][:], in0=sb["xcv"][:, et, :],
            scalar=sb["dcol"][:, et:et + 1], in1=ps[7][:, 0:TC],
            op0=ALU.mult, op1=ALU.add),
            reads=[f"xcv{et}", "dcol", PB[7]], writes=[y1b_], label="y1")

    def yf(c, et):
        y1b_ = f"y1{'ab'[et % 2]}"
        p.emit("pool", lambda eng, et=et, y1b_=y1b_: eng.tensor_tensor(
            out=sb["yfs"][:, et, :], in0=sb[y1b_][:], in1=sb["zs"][:, et, :],
            op=ALU.mult),
            reads=[y1b_, f"zs{et}"], writes=[f"yfs{et}"], label="yf")

    def ld_wout(k):
        wb = f"wout{k % 2}"
        p.emit("sp", lambda eng, k=k, wb=wb: eng.dma_start(
            out=sb[wb][:], in_=woutB[k]),
            reads=[f"wpco{k}"], writes=[wb], dma=True, chan="wo" + str(k % 2),
            label="ld_wo")

    st_toks = []

    def out_phase(co):
        t0 = co * TC
        for w, (dch, j) in enumerate(((0, 0), (0, 1), (1, 0), (1, 1))):
            for k in range(16):
                ld_wout(k)
                wb = f"wout{k % 2}"
                kw = {}
                if k == 0:
                    kw["writes"] = ["p5"]
                elif k == 15:
                    kw["nowait_writes"] = ["p5"]
                p.emit("pe", lambda eng, k=k, j=j, wb=wb, dch=dch:
                       eng.matmul(
                           ps[5][:, :],
                           sb["yfs"][:, k, j * 128:(j + 1) * 128],
                           sb[wb][:, dch * 512:(dch + 1) * 512],
                           start=(k == 0), stop=(k == 15)),
                       reads=[f"yfs{k}", wb], label="outp", **kw)
            sv = f"sev{w % 2}"
            p.emit("act", lambda eng, w=w: eng.activation(
                out=sb["sevac"][:, (w % 2) * 512:(w % 2) * 512 + 512],
                in_=ps[5][:, :], func=AF.Copy),
                reads=["p5"], writes=[sv], label="ev_s")
            tok = p.emit("sp", lambda eng, t0=t0, j=j, dch=dch, w=w: eng.dma_start(
                out=s_o[t0 + j * 128:t0 + (j + 1) * 128,
                        dch * 512:(dch + 1) * 512],
                in_=sb["sevac"][:, (w % 2) * 512:(w % 2) * 512 + 512]),
                reads=[sv], dma=True, chan="st_s", label="st_s")
            st_toks.append(tok)

    def iteration(c, et, last):
        n1 = c + 1
        if et + 2 <= 15:
            emit_dtmm(c, et + 2)
        if not last:
            if et + 1 <= 15:
                ld_winx(n1, et + 1)
            ld_winz(n1, et)
            inx(n1, et)
            if et >= 1:
                inz(n1, et - 1)
            if et >= 2:
                xcbxp(n1, et - 2)
        if et >= 1:
            yf(c, et - 1)
        if not last and et >= 1:
            conv_piece(n1, et - 1)
        aexp(c, et, 0)
        aexp(c, et, 1)
        for n in range(16):
            scan(c, et, n)
            if n == 7:
                hmul(c, et, 0)
            if n + 2 < 16:
                aexp(c, et, n + 2)
            if n == 2 and not last:
                pass
            if n == 3 and et + 1 <= 15:
                sp_stage(c, et + 1)
            if n == 6 and not last and et >= 1:
                silupair(n1, et - 1)
        hmul(c, et, 1)
        ymm_y1(c, et)
        if et + 1 <= 15:
            pool_stage(c, et + 1)
            cmul_stage(c, et + 1)

    def post_loop(c):
        n1 = c + 1
        yf(c, 15)
        inz(n1, 15)
        conv_piece(n1, 15)
        silupair(n1, 15)
        xcbxp(n1, 14)
        xcbxp(n1, 15)
        xp_evacs(n1)
        sel(n1)
        emit_dtmm(n1, 0)
        emit_dtmm(n1, 1)
        sp_stage(n1, 0)
        pool_stage(n1, 0)
        cmul_stage(n1, 0)
        if n1 + 1 < NCH:
            ld_winx(n1 + 1, 0)
            ld_xn(n1 + 1)

    ld_xn(0)
    for et in range(16):
        ld_winx(0, et)
        inx(0, et)
    for k in range(16):
        ld_winz(0, k)
        conv_piece(0, k)
        inz(0, k)
        silupair(0, k)
    for k in range(16):
        xcbxp(0, k)
    xp_evacs(0)
    sel(0)
    emit_dtmm(0, 0)
    emit_dtmm(0, 1)
    sp_stage(0, 0)
    pool_stage(0, 0)
    cmul_stage(0, 0)
    ld_winx(1, 0)
    ld_xn(1)

    for c in range(NCH):
        last = (c + 1 >= NCH)
        if c >= 1:
            out_phase(c - 1)
        if not last:
            p.emit("dve", lambda eng: eng.tensor_copy(
                sb["xsl"][:, :, 0:3], sb["xsl"][:, :, TC:TC + 3]),
                reads=xsl_m, writes=["xsl_h"], label="hist")
        for et in range(16):
            iteration(c, et, last)
        if not last:
            post_loop(c)
        else:
            yf(c, 15)
    out_phase(NCH - 1)
    p.emit("act", lambda eng: eng.activation(
        out=sb["mrow"][0:1, :], in_=sb["mrow"][0:1, :], func=AF.Copy),
        reads=["mrow"], deps=list(st_toks[-4:]), label="drain")

    with nc.Block() as block:
        p.build(block)
    es.close()
    return nc


def _xpt_pad(xpw):
    a = np.ascontiguousarray(xpw.T).reshape(16, 128, 96).astype(np.float32)
    out = np.zeros((16, 128, 112), np.float32)
    out[:, :, 0:64] = a[:, :, 0:64]
    out[:, :, 64:80] = a[:, :, 64:80]
    out[:, :, 96:112] = a[:, :, 80:96]
    return out


def _l1_inputs(x_b, ipw, cw, cb, xpw, dtw, dtb, alog, dvec, opw, wg_):
    f32 = np.float32
    onesm, identf, ibf, eall = _consts_np()
    return {
        "xT": np.ascontiguousarray(x_b.T).reshape(8, 128, L).astype(f32),
        "winT": np.ascontiguousarray(
            ipw.T.reshape(8, 128, 32, 128).transpose(2, 1, 0, 3)).astype(f32),
        "woutT": np.ascontiguousarray(opw.T).reshape(16, 128, D).astype(f32),
        "xpT": _xpt_pad(xpw),
        "dtT": np.ascontiguousarray(dtw.T).astype(f32),
        "wg": np.ascontiguousarray(wg_).reshape(8, 128, 8).astype(f32),
        "acol": np.ascontiguousarray(
            alog.reshape(16, 128, 16).transpose(1, 0, 2).reshape(128, 256)).astype(f32),
        "dcol": np.ascontiguousarray(dvec.reshape(16, 128).T).astype(f32),
        "cbcol": np.ascontiguousarray(cb.reshape(16, 128).T).astype(f32),
        "dtbcol": np.ascontiguousarray(dtb.reshape(16, 128).T).astype(f32),
        "cwcol": np.ascontiguousarray(
            cw.reshape(16, 128, 4).transpose(1, 0, 2).reshape(128, 64)).astype(f32),
        "onesm": onesm, "eall": eall, "ibf": ibf,
    }


TRASH = 100000.0


def build_l2():
    nc = bass.Bass("TRN2", target_bir_lowering=False, debug=False)
    lg_i = nc.dram_tensor("lg", [128, 128, 8], F32, kind="ExternalInput").ap()
    xnf_i = nc.dram_tensor("xnf", [T, D], BF16, kind="ExternalInput").ap()
    wgt_i = nc.dram_tensor("wgt", [8, 128, FF], F32, kind="ExternalInput").ap()
    wdn_i = nc.dram_tensor("wdn", [32, 128, D], F32, kind="ExternalInput").ap()
    onesm_i = nc.dram_tensor("onesm", [128, 128], F32, kind="ExternalInput").ap()
    identf_i = nc.dram_tensor("identf", [128, 128], F32, kind="ExternalInput").ap()
    ibf_i = nc.dram_tensor("ibf", [128, 128], BF16, kind="ExternalInput").ap()
    iota8_i = nc.dram_tensor("iota8", [128, 8], F32, kind="ExternalInput").ap()
    c8b_i = nc.dram_tensor("c8b", [128, 8], F32, kind="ExternalInput").ap()
    ccol_i = nc.dram_tensor("ccol", [128, 4], F32, kind="ExternalInput").ap()
    io128_i = nc.dram_tensor("io128", [128, 128], F32, kind="ExternalInput").ap()
    io32_i = nc.dram_tensor("io32", [128, 32], F32, kind="ExternalInput").ap()
    thr32_i = nc.dram_tensor("thr32", [128, 32], F32, kind="ExternalInput").ap()
    iotac_i = nc.dram_tensor("iotac", [128, 1], F32, kind="ExternalInput").ap()

    eo_o = nc.dram_tensor("eo_o", [CAP, D], F32, kind="ExternalOutput").ap()
    rt_o = nc.dram_tensor("rt_o", [128, 4, 128], F32, kind="ExternalOutput").ap()
    wgt_bf = nc.dram_tensor("wgt_bf", [8, 128, FF], BF16).ap()
    wdn_bf = nc.dram_tensor("wdn_bf", [32, 128, D], BF16).ap()

    es = ExitStack()
    sb = {}
    for name, shape, dt in [
        ("lg", [128, 1024], F32),
        ("onesm", [128, 128], F32),
        ("eall", [16, 16 * 128], F32), ("identf", [128, 128], F32),
        ("ibf", [128, 128], BF16),
        ("iota8", [128, 8], F32), ("c8b", [128, 8], F32), ("ccol", [128, 4], F32),
        ("io128", [128, 128], F32), ("io32", [128, 32], F32),
        ("thr32", [128, 32], F32), ("iotac", [128, 1], F32),
        ("m1", [128, 128], F32), ("m2", [128, 128], F32),
        ("e1", [128, 128], F32), ("e2", [128, 128], F32),
        ("mk1", [128, 1024], F32), ("mk2", [128, 1024], F32),
        ("lno", [128, 1024], F32),
        ("cnd", [128, 1024], F32),
        ("cum1", [128, 1024], F32), ("cum2", [128, 1024], F32),
        ("tot", [128, 8], F32), ("c1t", [8, 1], F32),
        ("t1T", [8, 128], F32), ("t2T", [8, 128], F32),
        ("o1T", [8, 128], F32), ("o2T", [8, 128], F32),
        ("off1", [128, 8], F32), ("off2", [128, 8], F32),
        ("g1", [128, 128], F32), ("g2", [128, 128], F32),
        ("l1v", [128, 128], F32), ("l2v", [128, 128], F32),
        ("k1", [128, 128], F32), ("k2", [128, 128], F32),
        ("f1g", [128, 128], F32), ("f2g", [128, 128], F32),
        ("d12", [128, 128], F32),
        ("cmy", [128, 128], F32), ("cmy2", [128, 128], F32),
        ("clo", [128, 128], F32), ("chi", [128, 128], F32),
        ("chiT", [128, 128], F32), ("cloT", [128, 128], F32),
        ("big3", [128, 4096], F32),
        ("ihi", [128, 128], BF16), ("ilo", [128, 32], BF16),
        ("vh", [128, 32], BF16), ("vl", [128, 32], BF16),
        ("tabh", [128, 32], F32), ("tabi", [128, 32], F32),
        ("idxt", [128, 32], I32),
        ("bigt", [128, 128], F32),
        ("dsp0", [128, D], BF16), ("dsp1", [128, D], BF16),
        ("dspT", [128, 8, CAP], BF16),
        ("wga", [128, 8, 128], BF16), ("wgb", [128, 8, 128], BF16),
        ("wda", [128, D], BF16), ("wdb", [128, D], BF16),
        ("hbf", [128, 32, 512], BF16),
        ("eoev", [128, 4, D], F32),
    ]:
        sb[name] = es.enter_context(nc.sbuf_tensor("s_" + name, shape, dt))

    ps = [es.enter_context(nc.psum_tensor(f"ps{i}", [128, 512], F32))
          for i in range(8)]
    PB = [f"bank{i}" for i in range(8)]

    p = Prog(nc, es)

    def load(name, dst_ap, src_ap, cast=False, chan=None, wbuf=None, deps=()):
        eng = "pool" if cast else "sp"
        return p.emit(eng, lambda eng_, d=dst_ap, s=src_ap: eng_.dma_start(out=d, in_=s),
                      writes=[wbuf or name], dma=True, chan=chan, deps=deps,
                      label="ld_" + name)

    for nm, srcap in [("lg", lg_i.rearrange("p f e -> p f e")), ("onesm", onesm_i),
                      ("identf", identf_i), ("ibf", ibf_i), ("iota8", iota8_i),
                      ("c8b", c8b_i), ("ccol", ccol_i), ("io128", io128_i),
                      ("io32", io32_i), ("thr32", thr32_i), ("iotac", iotac_i)]:
        dst = sb[nm][:]
        if nm == "lg":
            dst = sb["lg"][:].rearrange("p (f e) -> p f e", e=8)
        load(nm, dst, srcap)
    p.emit("dve", lambda eng: eng.memset(sb["bigt"][:], BIG), writes=["bigt"])
    for i in range(4):
        p.emit("pool", lambda eng, i=i: eng.dma_start(
            out=wgt_bf[2 * i:2 * i + 2], in_=wgt_i[2 * i:2 * i + 2]),
            writes=[f"wgtbf{i}"], dma=True, chan=f"wc{i % 2}", label="cast_wg")
        p.emit("pool", lambda eng, i=i: eng.dma_start(
            out=wdn_bf[8 * i:8 * i + 8], in_=wdn_i[8 * i:8 * i + 8]),
            writes=[f"wdnbf{i}"], dma=True, chan=f"wc{i % 2}", label="cast_wd")

    lg_v = sb["lg"][:].rearrange("p (f e) -> p f e", e=8)

    def emaj(t):
        return sb[t][:].rearrange("p (e f) -> p e f", e=8).transpose([0, 2, 1])
    mk1_pfe = emaj("mk1"); mk2_pfe = emaj("mk2")
    lno_pfe = emaj("lno"); cnd_pfe = emaj("cnd")
    iota_b = sb["iota8"][:].unsqueeze(1).broadcast_to([128, 128, 8])
    c8b_b = sb["c8b"][:].unsqueeze(1).broadcast_to([128, 128, 8])

    p.emit("dve", lambda eng: eng.reduce_max(sb["m1"][:], lg_v, axis=AX.X),
           reads=["lg"], writes=["m1"], label="m1")
    p.emit("dve", lambda eng: eng.tensor_tensor(
        out=mk1_pfe, in0=lg_v, in1=sb["m1"][:].unsqueeze(2).broadcast_to([128, 128, 8]),
        op=ALU.is_equal), reads=["lg", "m1"], writes=["mk1"], label="mk1raw")
    p.emit("dve", lambda eng: eng.scalar_tensor_tensor(
        out=cnd_pfe, in0=mk1_pfe, scalar=-BIG, in1=c8b_b,
        op0=ALU.mult, op1=ALU.add),
        reads=["mk1", "c8b"], writes=["cnd"], label="cnd1")
    p.emit("dve", lambda eng: eng.tensor_reduce(
        sb["e1"][:], cnd_pfe, axis=AX.X, op=ALU.min),
        reads=["cnd"], writes=["e1"], label="e1")
    p.emit("dve", lambda eng: eng.tensor_tensor(
        out=mk1_pfe, in0=iota_b, in1=sb["e1"][:].unsqueeze(2).broadcast_to([128, 128, 8]),
        op=ALU.is_equal), reads=["iota8", "e1", "mk1"], writes=["mk1"], label="mk1")
    p.emit("dve", lambda eng: eng.scalar_tensor_tensor(
        out=lno_pfe, in0=mk1_pfe, scalar=-BIG, in1=lg_v,
        op0=ALU.mult, op1=ALU.add),
        reads=["mk1", "lg"], writes=["lno"], label="lno")
    p.emit("dve", lambda eng: eng.reduce_max(sb["m2"][:], lno_pfe, axis=AX.X),
           reads=["lno"], writes=["m2"], label="m2")
    p.emit("dve", lambda eng: eng.tensor_tensor(
        out=cnd_pfe, in0=lno_pfe, in1=sb["m2"][:].unsqueeze(2).broadcast_to([128, 128, 8]),
        op=ALU.is_equal), reads=["lno", "m2", "cnd"], writes=["cnd"], label="mk2raw")
    p.emit("dve", lambda eng: eng.scalar_tensor_tensor(
        out=cnd_pfe, in0=cnd_pfe, scalar=-BIG, in1=c8b_b,
        op0=ALU.mult, op1=ALU.add),
        reads=["cnd", "c8b"], writes=["cnd"], label="cnd2")
    p.emit("dve", lambda eng: eng.tensor_reduce(
        sb["e2"][:], cnd_pfe, axis=AX.X, op=ALU.min),
        reads=["cnd"], writes=["e2"], label="e2")
    p.emit("dve", lambda eng: eng.tensor_tensor(
        out=mk2_pfe, in0=iota_b, in1=sb["e2"][:].unsqueeze(2).broadcast_to([128, 128, 8]),
        op=ALU.is_equal), reads=["iota8", "e2"], writes=["mk2"], label="mk2")
    p.emit("dve", lambda eng: eng.tensor_sub(sb["d12"][:], sb["m1"][:], sb["m2"][:]),
           reads=["m1", "m2"], writes=["d12"], label="d12")
    p.emit("act", lambda eng: eng.activation(
        out=sb["g1"][:], in_=sb["d12"][:], func=AF.Sigmoid),
        reads=["d12"], writes=["g1"], label="g1")
    p.emit("act", lambda eng: eng.activation(
        out=sb["g2"][:], in_=sb["d12"][:], func=AF.Sigmoid, scale=-1.0),
        reads=["d12"], writes=["g2"], label="g2")

    for mk, cum in (("mk1", "cum1"), ("mk2", "cum2")):
        for e in range(8):
            p.emit("dve", lambda eng, mk=mk, cum=cum, e=e: eng.tensor_tensor_scan(
                out=sb[cum][:, e * 128:(e + 1) * 128],
                data0=sb["onesm"][:],
                data1=sb[mk][:, e * 128:(e + 1) * 128],
                initial=0.0, op0=ALU.mult, op1=ALU.add),
                reads=[mk, "onesm"], writes=[f"{cum}_{e}"], label="cum")
    cum1_all = [f"cum1_{e}" for e in range(8)]
    cum2_all = [f"cum2_{e}" for e in range(8)]

    def cross_p(cum_all, cum, totb, tT, oT, off, extra=None):
        p.emit("dve", lambda eng: eng.tensor_copy(
            sb["tot"][:], sb[cum][:].rearrange("p (e f) -> p e f", e=8)[:, :, 127]),
            reads=cum_all, writes=[totb], label="tot")
        p.emit("pe", lambda eng: eng.transpose(
            ps[0][0:8, 0:128], sb["tot"][:], sb["identf"][:]),
            reads=[totb, "identf"], writes=[PB[0]], label="tp_tot")
        p.emit("act", lambda eng: eng.activation(
            out=sb[tT][:], in_=ps[0][0:8, 0:128], func=AF.Copy),
            reads=[PB[0]], writes=[tT], label="ev_tT")
        p.emit("dve", lambda eng: eng.tensor_tensor_scan(
            out=sb[oT][:], data0=sb["onesm"][0:8, :], data1=sb[tT][:],
            initial=0.0, op0=ALU.mult, op1=ALU.add),
            reads=[tT, "onesm"], writes=[oT], label="pscan")
        p.emit("dve", lambda eng: eng.tensor_sub(sb[oT][:], sb[oT][:], sb[tT][:]),
               reads=[oT, tT], writes=[oT], label="excl")
        if extra is not None:
            p.emit("dve", lambda eng: eng.tensor_tensor(
                out=sb[oT][:], in0=sb[oT][:], in1=extra, op=ALU.add),
                reads=[oT, "c1t"], writes=[oT], label="addc1")
        p.emit("pe", lambda eng: eng.transpose(
            ps[0][:, 0:8], sb[oT][:], sb["identf"][0:8, 0:8]),
            reads=[oT, "identf"], writes=[PB[0]], label="tp_off")
        p.emit("act", lambda eng: eng.activation(
            out=sb[off][:], in_=ps[0][:, 0:8], func=AF.Copy),
            reads=[PB[0]], writes=[off], label="ev_off")

    cross_p(cum1_all, "cum1", "tot1b", "t1T", "o1T", "off1")
    p.emit("dve", lambda eng: eng.tensor_add(
        sb["c1t"][:], sb["o1T"][:, 127:128], sb["t1T"][:, 127:128]),
        reads=["o1T", "t1T"], writes=["c1t"], label="c1tot")
    cross_p(cum2_all, "cum2", "tot2b", "t2T", "o2T", "off2",
            extra=sb["c1t"][:].broadcast_to([8, 128]))

    def loc_pipeline(mk, cum, off, lv):
        cum_all = [f"{cum}_{e}" for e in range(8)]
        mk_v = sb[mk][:].rearrange("p (e f) -> p e f", e=8)
        cum_v = sb[cum][:].rearrange("p (e f) -> p e f", e=8)
        p.emit("dve", lambda eng: eng.scalar_tensor_tensor(
            out=cum_v, in0=mk_v, scalar=-1.0, in1=cum_v,
            op0=ALU.mult, op1=ALU.add),
            reads=[mk] + cum_all, writes=cum_all, label="excl2")
        p.emit("dve", lambda eng: eng.tensor_tensor(
            out=cum_v, in0=cum_v,
            in1=sb[off][:].transpose([0, 1]).unsqueeze(2).broadcast_to([128, 8, 128]),
            op=ALU.add),
            reads=cum_all + [off], writes=cum_all, label="addoff")
        p.emit("dve", lambda eng: eng.tensor_tensor(
            out=cum_v, in0=cum_v, in1=mk_v, op=ALU.mult),
            reads=cum_all + [mk], writes=cum_all, label="selmk")
        p.emit("dve", lambda eng: eng.tensor_reduce(
            sb[lv][:], cum_v.transpose([0, 2, 1]), axis=AX.X, op=ALU.add),
            reads=cum_all, writes=[lv], label="locsum")

    loc_pipeline("mk1", "cum1", "off1", "l1v")
    loc_pipeline("mk2", "cum2", "off2", "l2v")

    for lv, ev, kv, fg, gv in (("l1v", "e1", "k1", "f1g", "g1"),
                               ("l2v", "e2", "k2", "f2g", "g2")):
        p.emit("dve", lambda eng, lv=lv, kv=kv: eng.tensor_single_scalar(
            sb[kv][:], sb[lv][:], float(CAP), ALU.is_lt),
            reads=[lv], writes=[kv], label="keep")
        p.emit("dve", lambda eng, lv=lv: eng.tensor_scalar_min(
            sb[lv][:], sb[lv][:], float(CAP - 1)),
            reads=[lv], writes=[lv], label="clamp")
        p.emit("dve", lambda eng, lv=lv, ev=ev, fg=fg: eng.scalar_tensor_tensor(
            out=sb[fg][:], in0=sb[ev][:], scalar=float(CAP), in1=sb[lv][:],
            op0=ALU.mult, op1=ALU.add),
            reads=[ev, lv], writes=[fg], label="fslot")
        p.emit("dve", lambda eng, fg=fg, kv=kv: eng.tensor_mul(
            sb[fg][:], sb[fg][:], sb[kv][:]),
            reads=[fg, kv], writes=[fg], label="fkeep")
        p.emit("dve", lambda eng, gv=gv, kv=kv: eng.tensor_mul(
            sb[gv][:], sb[gv][:], sb[kv][:]),
            reads=[gv, kv], writes=[gv], label="gk")

    for i, nm in enumerate(("f1g", "f2g", "g1", "g2")):
        p.emit("sp", lambda eng, i=i, nm=nm: eng.dma_start(
            out=rt_o[:, i, :], in_=sb[nm][:]),
            reads=[nm], dma=True, chan="rt", label="st_rt")

    ecol = sb["ccol"][:, 2:3]
    for ev, lv, kv, dst in (("e1", "l1v", "k1", "cmy"), ("e2", "l2v", "k2", "cmy2")):
        p.emit("dve", lambda eng, ev=ev: eng.tensor_scalar(
            out=sb["clo"][:], in0=sb[ev][:], scalar1=ecol, scalar2=None,
            op0=ALU.is_equal),
            reads=[ev, "ccol"], writes=["minetmp"], label="mine")
        p.emit("dve", lambda eng, kv=kv: eng.tensor_mul(
            sb["clo"][:], sb["clo"][:], sb[kv][:]),
            reads=["minetmp", kv], writes=["minetmp"], label="minek")
        p.emit("dve", lambda eng, lv=lv, dst=dst: eng.tensor_scalar_add(
            sb[dst][:], sb[lv][:], -BIG),
            reads=[lv], writes=[dst], label="cmysub")
        p.emit("dve", lambda eng, dst=dst: eng.tensor_mul(
            sb[dst][:], sb[dst][:], sb["clo"][:]),
            reads=[dst, "minetmp"], writes=[dst], label="cmymul")
        p.emit("dve", lambda eng, dst=dst: eng.tensor_scalar_add(
            sb[dst][:], sb[dst][:], BIG),
            reads=[dst], writes=[dst], label="cmyadd")
    p.emit("dve", lambda eng: eng.tensor_tensor(
        out=sb["cmy"][:], in0=sb["cmy"][:], in1=sb["cmy2"][:], op=ALU.min),
        reads=["cmy", "cmy2"], writes=["cmy"], label="cmymin")

    p.emit("dve", lambda eng: eng.tensor_tensor(
        out=sb["big3"][:].rearrange("p (f l) -> p f l", l=32),
        in0=sb["cmy"][:].unsqueeze(2).broadcast_to([128, 128, 32]),
        in1=sb["thr32"][:].unsqueeze(1).broadcast_to([128, 128, 32]),
        op=ALU.is_ge),
        reads=["cmy", "thr32"], writes=["big3b"], label="clocmp")
    p.emit("dve", lambda eng: eng.tensor_reduce(
        sb["clo"][:], sb["big3"][:].rearrange("p (f l) -> p f l", l=32),
        axis=AX.X, op=ALU.add),
        reads=["big3b"], writes=["clo"], label="closum")
    p.emit("dve", lambda eng: eng.scalar_tensor_tensor(
        out=sb["chi"][:], in0=sb["clo"][:], scalar=-128.0, in1=sb["cmy"][:],
        op0=ALU.mult, op1=ALU.add),
        reads=["clo", "cmy"], writes=["chi"], label="chi")
    p.emit("pe", lambda eng: eng.transpose(
        ps[0][:, 0:128], sb["chi"][:], sb["identf"][:]),
        reads=["chi", "identf"], writes=[PB[0]], label="tp_chi")
    p.emit("act", lambda eng: eng.activation(
        out=sb["chiT"][:], in_=ps[0][:, 0:128], func=AF.Copy),
        reads=[PB[0]], writes=["chiT"], label="ev_chiT")
    p.emit("pe", lambda eng: eng.transpose(
        ps[0][:, 0:128], sb["clo"][:], sb["identf"][:]),
        reads=["clo", "identf"], writes=[PB[0]], label="tp_clo")
    p.emit("act", lambda eng: eng.activation(
        out=sb["cloT"][:], in_=ps[0][:, 0:128], func=AF.Copy),
        reads=[PB[0]], writes=["cloT"], label="ev_cloT")

    for kt in range(128):
        p.emit("dve", lambda eng, kt=kt: eng.tensor_tensor(
            out=sb["ihi"][:], in0=sb["chiT"][:, kt:kt + 1].broadcast_to([128, 128]),
            in1=sb["io128"][:], op=ALU.is_equal),
            reads=["chiT", "io128"], writes=["ihi"], label="ihi")
        p.emit("dve", lambda eng, kt=kt: eng.tensor_tensor(
            out=sb["ilo"][:], in0=sb["cloT"][:, kt:kt + 1].broadcast_to([128, 32]),
            in1=sb["io32"][:], op=ALU.is_equal),
            reads=["cloT", "io32"], writes=["ilo"], label="ilo")
        p.emit("dve", lambda eng, kt=kt: eng.tensor_scalar_mul(
            sb["vh"][:], sb["ilo"][:], float(kt)),
            reads=["ilo"], writes=["vh"], label="vh")
        p.emit("dve", lambda eng: eng.tensor_scalar_mul(
            sb["vl"][:], sb["ilo"][:], sb["iotac"][:, 0:1]),
            reads=["ilo", "iotac"], writes=["vl"], label="vl")
        for bi, rhs, rb in ((1, "vh", "vh"), (2, "vl", "vl"), (3, "ilo", "ilo")):
            kw = {}
            if kt == 0:
                kw["writes"] = [PB[bi]]
            elif kt == 127:
                kw["nowait_writes"] = [PB[bi]]
            p.emit("pe", lambda eng, kt=kt, bi=bi, rhs=rhs: eng.matmul(
                ps[bi][:, 0:32], sb["ihi"][:], sb[rhs][:],
                start=(kt == 0), stop=(kt == 127)),
                reads=["ihi", rb], label="tabmm", **kw)
    p.emit("act", lambda eng: eng.activation(
        out=sb["tabh"][:], in_=ps[1][:, 0:32], func=AF.Copy),
        reads=[PB[1]], writes=["tabh"], label="ev_tabh")
    p.emit("dve", lambda eng: eng.scalar_tensor_tensor(
        out=sb["tabh"][:], in0=sb["tabh"][:], scalar=128.0, in1=ps[2][:, 0:32],
        op0=ALU.mult, op1=ALU.add),
        reads=["tabh", PB[2]], writes=["tabh"], label="tabcomb")
    p.emit("act", lambda eng: eng.activation(
        out=sb["tabi"][:], in_=ps[3][:, 0:32], func=AF.Copy),
        reads=[PB[3]], writes=["tabi"], label="ev_fill")
    p.emit("dve", lambda eng: eng.tensor_scalar_add(
        sb["tabh"][:], sb["tabh"][:], -BIG),
        reads=["tabh"], writes=["tabh"], label="tabsub")
    p.emit("dve", lambda eng: eng.tensor_mul(
        sb["tabh"][:], sb["tabh"][:], sb["tabi"][:]),
        reads=["tabh", "tabi"], writes=["tabh"], label="tabmul")
    p.emit("dve", lambda eng: eng.tensor_scalar_add(
        sb["tabh"][:], sb["tabh"][:], BIG),
        reads=["tabh"], writes=["tabh"], label="tabadd")
    p.emit("dve", lambda eng: eng.tensor_copy(sb["idxt"][:], sb["tabh"][:]),
           reads=["tabh"], writes=["idxt"], label="idxint")

    for cc in range(32):
        db = f"dsp{cc % 2}"
        p.emit("dve", lambda eng, db=db: eng.memset(sb[db][:], 0.0),
               writes=[db], label="dsp0")
        p.emit("pool", lambda eng, db=db, cc=cc: eng.indirect_dma_start(
            out=sb[db][:, :], out_offset=None, in_=xnf_i,
            in_offset=bass.IndirectOffsetOnAxis(sb["idxt"][:, cc:cc + 1], 0),
            bounds_check=T - 1, oob_is_err=False),
            reads=["idxt", db], writes=[db], dma=True, chan=f"g{cc % 2}",
            label="dgather")
        for kt in range(8):
            p.emit("pe", lambda eng, db=db, kt=kt: eng.transpose(
                ps[4][:, 0:128].bitcast(BF16)[:, 0:128],
                sb[db][:, kt * 128:(kt + 1) * 128], sb["ibf"][:]),
                reads=[db, "ibf"], writes=[PB[4]], label="tp_dsp")
            p.emit("act", lambda eng, cc=cc, kt=kt: eng.activation(
                out=sb["dspT"][:, kt, cc * 128:(cc + 1) * 128],
                in_=ps[4][:, 0:128].bitcast(BF16)[:, 0:128], func=AF.Copy),
                reads=[PB[4]], writes=[f"dspT{kt}"], label="ev_dspT")
    dspT_all = [f"dspT{kt}" for kt in range(8)]

    for tc8 in range(8):
        tt0 = tc8 * 512
        for fs in range(32):
            wb = "wga" if fs % 2 == 0 else "wgb"
            p.emit("sp", lambda eng, wb=wb, fs=fs: eng.dma_start(
                out=sb[wb][:],
                in_=wgt_bf.rearrange("k p f -> p k f")[:, :, fs * 128:(fs + 1) * 128]),
                reads=[f"wgtbf{i}" for i in range(4)], writes=[wb], dma=True,
                chan=wb, label="ld_wga")
            bank, pb = (ps[5], PB[5]) if fs % 2 == 0 else (ps[6], PB[6])
            p.mm_acc([(lambda eng, wb=wb, kt=kt, tt0=tt0, bank=bank: eng.matmul(
                bank[:, :], sb[wb][:, kt, :], sb["dspT"][:, kt, tt0:tt0 + 512],
                start=(kt == 0), stop=(kt == 7)),
                [wb] + dspT_all) for kt in range(8)], pb, label="gate")
            p.emit("act", lambda eng, fs=fs, bank=bank, pb=pb: eng.activation(
                out=sb["hbf"][:, fs, :], in_=bank[:, :], func=AF.Gelu),
                reads=[pb], writes=[f"hbf{fs}"], label="gelu")
        for fs in range(32):
            wb = "wda" if fs % 2 == 0 else "wdb"
            p.emit("sp", lambda eng, wb=wb, fs=fs: eng.dma_start(
                out=sb[wb][:],
                in_=wdn_bf[fs:fs + 1].rearrange("a p d -> p a d").squeeze(1)),
                reads=[f"wdnbf{i}" for i in range(4)], writes=[wb], dma=True,
                chan=wb, label="ld_wda")
            for j in range(4):
                for dch in range(2):
                    bi = 4 * dch + j
                    if bi >= 8:
                        continue
                    kw = {}
                    if fs == 0:
                        kw["writes"] = [PB[bi]]
                    elif fs == 31:
                        kw["nowait_writes"] = [PB[bi]]
                    p.emit("pe", lambda eng, fs=fs, j=j, dch=dch, wb=wb, bi=bi:
                           eng.matmul(
                               ps[bi][:, :],
                               sb["hbf"][:, fs, j * 128:(j + 1) * 128],
                               sb[wb][:, dch * 512:(dch + 1) * 512],
                               start=(fs == 0), stop=(fs == 31)),
                           reads=[f"hbf{fs}", wb], label="down", **kw)
        for j in range(4):
            for dch in range(2):
                bi = 4 * dch + j
                p.emit("act", lambda eng, j=j, dch=dch, bi=bi: eng.activation(
                    out=sb["eoev"][:, j, dch * 512:(dch + 1) * 512],
                    in_=ps[bi][:, :], func=AF.Copy),
                    reads=[PB[bi]], writes=[f"eoev{j}"], label="ev_eo")
        for j in range(4):
            p.emit("sp", lambda eng, tt0=tt0, j=j: eng.dma_start(
                out=eo_o[tt0 + j * 128:tt0 + (j + 1) * 128, :],
                in_=sb["eoev"][:, j, :]),
                reads=[f"eoev{j}"], dma=True, chan="st_eo", label="st_eo")

    with nc.Block() as block:
        p.build(block)
    es.close()
    return nc


def _l2_inputs(e, lg_full, xn_full, w_gate_e, w_down_e):
    f32 = np.float32
    onesm, identf, ibf, eall = _consts_np()
    iota8 = np.tile(np.arange(8, dtype=f32), (128, 1))
    c8b = iota8 + BIG
    ccol = np.zeros((128, 4), f32)
    ccol[:, 0] = e * CAP
    ccol[:, 1] = (e + 1) * CAP
    ccol[:, 2] = e
    io128 = np.tile(np.arange(128, dtype=f32), (128, 1))
    io32 = np.tile(np.arange(32, dtype=f32), (128, 1))
    thr32 = np.tile(128.0 * np.arange(1, 33, dtype=f32), (128, 1))
    iotac = np.arange(128, dtype=f32).reshape(128, 1)
    return {
        "lg": np.ascontiguousarray(lg_full.reshape(128, 128, 8)).astype(f32),
        "xnf": xn_full,
        "wgt": np.ascontiguousarray(w_gate_e).reshape(8, 128, FF).astype(f32),
        "wdn": np.ascontiguousarray(w_down_e).reshape(32, 128, D).astype(f32),
        "onesm": onesm, "identf": identf, "ibf": ibf,
        "iota8": iota8, "c8b": c8b, "ccol": ccol,
        "io128": io128, "io32": io32, "thr32": thr32, "iotac": iotac,
    }


def build_l3():
    nc = bass.Bass("TRN2", target_bir_lowering=False, debug=False)
    eof_i = nc.dram_tensor("eof", [NEXP * CAP, D], F32, kind="ExternalInput").ap()
    rtf_i = nc.dram_tensor("rtf", [128, 4, 16], F32, kind="ExternalInput").ap()
    x_i = nc.dram_tensor("xb", [L, D], F32, kind="ExternalInput").ap()
    s_i = nc.dram_tensor("sb", [L, D], F32, kind="ExternalInput").ap()
    out_o = nc.dram_tensor("out_o", [L, D], F32, kind="ExternalOutput").ap()

    es = ExitStack()
    sb = {}
    names = [("rtf", [128, 4, 16], F32), ("i1", [128, 16], I32), ("i2", [128, 16], I32)]
    for par in (0, 1):
        names += [(f"eo1_{par}", [128, D], F32), (f"eo2_{par}", [128, D], F32),
                  (f"xt_{par}", [128, D], F32), (f"st_{par}", [128, D], F32),
                  (f"t1_{par}", [128, D], F32), (f"gg_{par}", [128, D], F32),
                  (f"ot_{par}", [128, D], F32)]
    for name, shape, dt in names:
        sb[name] = es.enter_context(nc.sbuf_tensor("s_" + name, shape, dt))

    p = Prog(nc, es)

    p.emit("sp", lambda eng: eng.dma_start(out=sb["rtf"][:], in_=rtf_i),
           writes=["rtf"], dma=True, label="ld_rtf")
    p.emit("dve", lambda eng: eng.tensor_copy(sb["i1"][:], sb["rtf"][:, 0, :]),
           reads=["rtf"], writes=["i1"], label="i1")
    p.emit("dve", lambda eng: eng.tensor_copy(sb["i2"][:], sb["rtf"][:, 1, :]),
           reads=["rtf"], writes=["i2"], label="i2")

    for q in range(16):
        par = q % 2
        eo1, eo2, xt, st, t1, gg, ot = (f"eo1_{par}", f"eo2_{par}", f"xt_{par}",
                                        f"st_{par}", f"t1_{par}", f"gg_{par}",
                                        f"ot_{par}")
        p.emit("dve", lambda eng, eo1=eo1: eng.memset(sb[eo1][:], 0.0), writes=[eo1])
        p.emit("dve", lambda eng, eo2=eo2: eng.memset(sb[eo2][:], 0.0), writes=[eo2])
        p.emit("pool", lambda eng, q=q, eo1=eo1: eng.indirect_dma_start(
            out=sb[eo1][:, :], out_offset=None, in_=eof_i,
            in_offset=bass.IndirectOffsetOnAxis(sb["i1"][:, q:q + 1], 0),
            bounds_check=NEXP * CAP - 1, oob_is_err=False),
            reads=["i1", eo1], writes=[eo1], dma=True, chan=f"g1{par}", label="g1")
        p.emit("pool", lambda eng, q=q, eo2=eo2: eng.indirect_dma_start(
            out=sb[eo2][:, :], out_offset=None, in_=eof_i,
            in_offset=bass.IndirectOffsetOnAxis(sb["i2"][:, q:q + 1], 0),
            bounds_check=NEXP * CAP - 1, oob_is_err=False),
            reads=["i2", eo2], writes=[eo2], dma=True, chan=f"g2{par}", label="g2")
        p.emit("sp", lambda eng, q=q, xt=xt: eng.dma_start(
            out=sb[xt][:], in_=x_i[q * 128:(q + 1) * 128, :]),
            writes=[xt], dma=True, chan=f"x{par}", label="ld_x")
        p.emit("sp", lambda eng, q=q, st=st: eng.dma_start(
            out=sb[st][:], in_=s_i[q * 128:(q + 1) * 128, :]),
            writes=[st], dma=True, chan=f"s{par}", label="ld_s")
        p.emit("dve", lambda eng, q=q, t1=t1, eo1=eo1: eng.tensor_scalar_mul(
            sb[t1][:], sb[eo1][:], sb["rtf"][:, 2, q:q + 1]),
            reads=[eo1, "rtf"], writes=[t1], label="gmul1")
        p.emit("dve", lambda eng, q=q, gg=gg, eo2=eo2, t1=t1: eng.scalar_tensor_tensor(
            out=sb[gg][:], in0=sb[eo2][:], scalar=sb["rtf"][:, 3, q:q + 1],
            in1=sb[t1][:], op0=ALU.mult, op1=ALU.add),
            reads=[eo2, "rtf", t1], writes=[gg], label="gmul2")
        p.emit("act", lambda eng, gg=gg: eng.activation(
            out=sb[gg][:], in_=sb[gg][:], func=AF.Silu),
            reads=[gg], writes=[gg], label="silu")
        p.emit("dve", lambda eng, t1=t1, st=st, gg=gg: eng.tensor_mul(
            sb[t1][:], sb[st][:], sb[gg][:]),
            reads=[st, gg], writes=[t1], label="sg")
        p.emit("dve", lambda eng, ot=ot, xt=xt, t1=t1: eng.tensor_add(
            sb[ot][:], sb[xt][:], sb[t1][:]),
            reads=[xt, t1], writes=[ot], label="outadd")
        p.emit("sp", lambda eng, q=q, ot=ot: eng.dma_start(
            out=out_o[q * 128:(q + 1) * 128, :], in_=sb[ot][:]),
            reads=[ot], dma=True, chan=f"o{par}", label="st_o")

    with nc.Block() as block:
        p.build(block)
    es.close()
    return nc


def _l3_inputs(b, eof, rt, x_b, s_b):
    f32 = np.float32
    rt = np.asarray(rt).reshape(128, 4, 128)
    mine = rt[16 * b:16 * b + 16]
    rtf = np.ascontiguousarray(mine.transpose(2, 1, 0))
    return {
        "eof": eof.astype(f32), "rtf": rtf.astype(f32),
        "xb": x_b.astype(f32), "sb": s_b.astype(f32),
    }


def kernel(L0_attn_hidden_states, L0_attn_in_proj_w, L0_attn_conv_w,
           L0_attn_conv_b, L0_attn_x_proj_w, L0_attn_dt_proj_w, L0_attn_dt_proj_b,
           L0_attn_A_log, L0_attn_D, L0_attn_out_proj_w, L0_ffn_wg, L0_ffn_w_gate,
           L0_ffn_w_down):
    x = np.asarray(L0_attn_hidden_states)
    core_ids = list(range(NC))

    nc1 = build_l1()
    maps1 = [_l1_inputs(x[b], L0_attn_in_proj_w, L0_attn_conv_w, L0_attn_conv_b,
                        L0_attn_x_proj_w, L0_attn_dt_proj_w, L0_attn_dt_proj_b,
                        L0_attn_A_log, L0_attn_D, L0_attn_out_proj_w, L0_ffn_wg)
             for b in range(B)]
    res1 = run_bass_kernel_spmd(nc1, maps1, core_ids=core_ids).results

    lg_full = np.concatenate([res1[b]["lgT_o"].T for b in range(B)], axis=0)
    xn_full = np.concatenate(
        [np.ascontiguousarray(np.asarray(res1[b]["xnT_o"]).reshape(D, L).T)
         for b in range(B)], axis=0)
    s_full = [res1[b]["s_o"] for b in range(B)]

    nc2 = build_l2()
    maps2 = [_l2_inputs(e, lg_full, xn_full, np.asarray(L0_ffn_w_gate)[e],
                        np.asarray(L0_ffn_w_down)[e]) for e in range(NEXP)]
    res2 = run_bass_kernel_spmd(nc2, maps2, core_ids=core_ids).results

    eof = np.concatenate([res2[e]["eo_o"] for e in range(NEXP)], axis=0)
    rt = res2[0]["rt_o"]

    nc3 = build_l3()
    maps3 = [_l3_inputs(b, eof, rt, x[b], s_full[b]) for b in range(B)]
    res3 = run_bass_kernel_spmd(nc3, maps3, core_ids=core_ids).results

    return np.stack([res3[b]["out_o"] for b in range(B)], axis=0)


# version 08
# speedup vs baseline: 1.2944x; 1.2944x over previous
from collections import defaultdict
from contextlib import ExitStack

import numpy as np
import ml_dtypes
import concourse.bass as bass
import concourse.mybir as mybir
from concourse.bass_utils import run_bass_kernel_spmd

F32 = mybir.dt.float32
BF16 = mybir.dt.bfloat16
I32 = mybir.dt.int32
AF = mybir.ActivationFunctionType
ALU = mybir.AluOpType
AX = mybir.AxisListType

NC = 8
B = 8
L = 2048
D = 1024
E = 2048
NST = 16
RK = 64
NEXP = 8
CAP = 4096
FF = 4096
T = B * L
TC = 256
NCH = L // TC
BIG = 1.0e7
LN_EPS = 1e-5
BF = ml_dtypes.bfloat16


class Bbuf:
    __slots__ = ("writers", "readers", "name")

    def __init__(self, name=""):
        self.writers = []
        self.readers = []
        self.name = name


class Prog:
    ENG_ATTR = {"sp": "sync", "act": "scalar", "dve": "vector",
                "pool": "gpsimd", "pe": "tensor"}
    ENGINES = ("sp", "act", "dve", "pool", "pe")
    N_DMA_CHANNELS = 6

    def __init__(self, nc, es):
        self.nc = nc
        self.es = es
        self.items = defaultdict(list)
        self.counts = defaultdict(int)
        self.sems = {}
        for e in self.ENGINES:
            self.sems["c_" + e] = es.enter_context(nc.semaphore("c_" + e))
        self._rot = defaultdict(int)
        self._chan_last = {}
        self.bufs = defaultdict(Bbuf)

    def _chan_sem(self, name):
        if name not in self.sems:
            self.sems[name] = self.es.enter_context(self.nc.semaphore(name))
        return name

    def emit(self, engine, fn, reads=(), writes=(), deps=(), dma=False, chan=None,
             label="", nowait_writes=()):
        reads = [self.bufs[r] if isinstance(r, str) else r for r in reads]
        writes = [self.bufs[w] if isinstance(w, str) else w for w in writes]
        nowait_writes = [self.bufs[w] if isinstance(w, str) else w
                         for w in nowait_writes]
        dep_toks = list(deps)
        for b in reads:
            dep_toks += b.writers
        for b in writes:
            dep_toks += b.writers + b.readers
        if dma:
            if chan is None:
                chan = f"{engine}{self._rot[engine] % self.N_DMA_CHANNELS}"
                self._rot[engine] += 1
            sem = self._chan_sem("q_" + chan)
            step = 16
            prev = self._chan_last.get(sem)
            if prev is not None:
                dep_toks.append(prev)
        else:
            sem, step = "c_" + engine, 1
        self.counts[sem] += step
        tok = (sem, self.counts[sem])
        self.items[engine].append((dep_toks, fn, sem, step, label))
        if dma:
            self._chan_last[sem] = tok
        for b in reads:
            b.readers.append(tok)
        for b in writes:
            b.writers = [tok]
            b.readers = []
        for b in nowait_writes:
            b.writers = [tok]
            b.readers = []
        return tok

    def mm_acc(self, calls, ps_buf, label=""):
        n = len(calls)
        tok = None
        for i, (fn, reads) in enumerate(calls):
            if n == 1:
                tok = self.emit("pe", fn, reads=reads, writes=[ps_buf], label=label)
            elif i == 0:
                tok = self.emit("pe", fn, reads=reads, writes=[ps_buf], label=label)
            elif i == n - 1:
                tok = self.emit("pe", fn, reads=reads, nowait_writes=[ps_buf],
                                label=label)
            else:
                tok = self.emit("pe", fn, reads=reads, label=label)
        return tok

    def build(self, block):
        waited = defaultdict(int)

        def make_section(engine):
            items = self.items[engine]

            def section(eng):
                for dep_toks, fn, sem, step, label in items:
                    need = {}
                    for (dsem, dval) in dep_toks:
                        if waited[(engine, dsem)] >= dval:
                            continue
                        if need.get(dsem, 0) < dval:
                            need[dsem] = dval
                    for dsem, dval in need.items():
                        eng.wait_ge(self.sems[dsem], dval)
                        waited[(engine, dsem)] = dval
                    inst = fn(eng)
                    inst.then_inc(self.sems[sem], step)

            return section

        for engine in self.ENGINES:
            if not self.items[engine]:
                continue
            getattr(block, self.ENG_ATTR[engine])(make_section(engine))


def _consts_np():
    onesm = np.ones((128, 128), np.float32)
    ibf = np.eye(128, dtype=BF)
    identf = np.eye(128, dtype=np.float32)
    eall = np.zeros((16, 16 * 128), np.float32)
    for n in range(16):
        eall[n, n * 128:(n + 1) * 128] = 1.0
    return onesm, identf, ibf, eall


def build_l1():
    nc = bass.Bass("TRN2", target_bir_lowering=False, debug=False)
    xT = nc.dram_tensor("xT", [8, 128, L], F32, kind="ExternalInput").ap()
    winT = nc.dram_tensor("winT", [32, 128, 8, 128], F32, kind="ExternalInput").ap()
    woutT = nc.dram_tensor("woutT", [16, 128, D], F32, kind="ExternalInput").ap()
    xpT = nc.dram_tensor("xpT", [16, 128, 112], F32, kind="ExternalInput").ap()
    dtT = nc.dram_tensor("dtT", [64, E], F32, kind="ExternalInput").ap()
    wg = nc.dram_tensor("wg", [8, 128, 8], F32, kind="ExternalInput").ap()
    eall_i = nc.dram_tensor("eall", [16, 16 * 128], F32, kind="ExternalInput").ap()
    acol = nc.dram_tensor("acol", [128, 256], F32, kind="ExternalInput").ap()
    dcol = nc.dram_tensor("dcol", [128, 16], F32, kind="ExternalInput").ap()
    cbcol = nc.dram_tensor("cbcol", [128, 16], F32, kind="ExternalInput").ap()
    dtbcol = nc.dram_tensor("dtbcol", [128, 16], F32, kind="ExternalInput").ap()
    cwcol = nc.dram_tensor("cwcol", [128, 64], F32, kind="ExternalInput").ap()
    onesm_i = nc.dram_tensor("onesm", [128, 128], F32, kind="ExternalInput").ap()
    ibf_i = nc.dram_tensor("ibf", [128, 128], BF16, kind="ExternalInput").ap()

    lgT_o = nc.dram_tensor("lgT_o", [8, L], F32, kind="ExternalOutput").ap()
    xnT_o = nc.dram_tensor("xnT_o", [NCH, 128, 8, TC], BF16, kind="ExternalOutput").ap()
    s_o = nc.dram_tensor("s_o", [L, D], F32, kind="ExternalOutput").ap()
    winB = nc.dram_tensor("winB", [32, 128, 8, 128], BF16).ap()
    woutB = nc.dram_tensor("woutB", [16, 128, D], BF16).ap()

    es = ExitStack()
    sb = {}
    for name, shape, dt in [
        ("arena", [128, 12288], F32),
        ("mrow", [1, 256], F32), ("erow", [1, 256], F32), ("rrow", [1, 256], F32),
        ("xnc0", [128, 8, TC], BF16), ("xnc1", [128, 8, TC], BF16),
        ("xmc0", [128, 8, TC], BF16), ("xmc1", [128, 8, TC], BF16),
        ("lgT", [8, 512], F32),
        ("wgs", [128, 64], F32),
        ("onesm", [128, 128], F32),
        ("eall", [16, 16 * 128], F32),
        ("ibf", [128, 128], BF16),
        ("acol", [128, 256], F32), ("aneg", [128, 256], F32),
        ("dcol", [128, 16], F32),
        ("cbcol", [128, 16], F32),
        ("dtbcol", [128, 16], F32),
        ("cwcol", [128, 64], F32),
        ("xpTs", [128, 16 * 112], BF16),
        ("dtTs", [64, E], BF16),
        ("winx0", [128, 8, 128], BF16), ("winx1", [128, 8, 128], BF16),
        ("winz0", [128, 8, 128], BF16), ("winz1", [128, 8, 128], BF16),
        ("wout0", [128, D], BF16), ("wout1", [128, D], BF16),
        ("wout2", [128, D], BF16), ("wout3", [128, D], BF16),
        ("xsl", [128, 16, TC + 3], F32),
        ("xcv", [128, 16, TC], F32),
        ("xcb0", [128, TC], BF16), ("xcb1", [128, TC], BF16),
        ("cvt0", [128, TC], F32), ("cvt1", [128, TC], F32),
        ("dtin", [64, TC], BF16),
        ("bcsb", [16, 512], F32),
        ("spr0", [128, TC], F32), ("spt0", [128, TC], F32),
        ("spr1", [128, TC], F32), ("spt1", [128, TC], F32),
        ("du0", [128, TC], F32), ("du1", [128, TC], F32),
        ("dlt0", [128, TC], F32), ("dlt1", [128, TC], F32),
        ("a0", [128, TC], F32), ("a1", [128, TC], F32),
        ("hall", [128, 16, TC], F32),
        ("hbf", [128, 16, TC], BF16),
        ("carry", [128, 256], F32),
        ("y1a", [128, TC], F32), ("y1b", [128, TC], F32),
        ("zs", [128, 16, TC], F32),
        ("yfs", [128, 16, TC], BF16),
        ("sevac", [128, D], F32),
    ]:
        sb[name] = es.enter_context(nc.sbuf_tensor("s_" + name, shape, dt))

    ar = sb["arena"]
    xt_v = [ar[:, 0:2048].rearrange("p (k t) -> p k t", k=8),
            ar[:, 2048:4096].rearrange("p (k t) -> p k t", k=8)]
    x2_v = ar[:, 4096:6144].rearrange("p (k t) -> p k t", k=8)
    xn32_v = ar[:, 6144:8192].rearrange("p (k t) -> p k t", k=8)
    bcall_q = ar[:, 0:8192].rearrange("p (n q) -> p n q", n=16)
    bball_v = bcall_q[:, :, 0:256]
    cball_v = bcall_q[:, :, 256:512]
    call_v = ar[:, 8192:12288].rearrange("p (n t) -> p n t", n=16)

    ps = [es.enter_context(nc.psum_tensor(f"ps{i}", [128, 512], F32))
          for i in range(8)]
    PB = [f"bank{i}" for i in range(8)]

    p = Prog(nc, es)

    def load(name, dst_ap, src_ap, cast=False, chan=None, wbuf=None):
        eng = "pool" if cast else "sp"
        return p.emit(eng, lambda eng_, d=dst_ap, s=src_ap: eng_.dma_start(out=d, in_=s),
                      writes=[wbuf or name], dma=True, chan=chan, label="ld_" + name)

    load("onesm", sb["onesm"][:], onesm_i)
    load("eall", sb["eall"][:], eall_i)
    load("ibf", sb["ibf"][:], ibf_i)
    load("acol", sb["acol"][:], acol)
    load("dcol", sb["dcol"][:], dcol)
    load("cbcol", sb["cbcol"][:], cbcol)
    load("dtbcol", sb["dtbcol"][:], dtbcol)
    load("cwcol", sb["cwcol"][:], cwcol)
    load("wgs", sb["wgs"][:].rearrange("p (k e) -> p k e", k=8),
         wg.rearrange("k p e -> p k e"))
    load("xpTs", sb["xpTs"][:].rearrange("p (t r) -> p t r", t=16),
         xpT.rearrange("t p r -> p t r"), cast=True)
    load("dtTs", sb["dtTs"][:], dtT, cast=True)
    p.emit("act", lambda eng: eng.activation(out=sb["aneg"][:], in_=sb["acol"][:],
                                             func=AF.Exp),
           reads=["acol"], writes=["aneg"], label="aexp0")
    p.emit("act", lambda eng: eng.activation(out=sb["aneg"][:], in_=sb["aneg"][:],
                                             func=AF.Copy, scale=-1.0),
           reads=["aneg"], writes=["aneg"], label="aneg")

    xT_p = xT.rearrange("k p t -> p k t")
    for cc in range(8):
        t0 = cc * 256
        xb = f"xt{cc % 2}"
        xv = xt_v[cc % 2]
        load(xb, xv, xT_p[:, :, t0:t0 + 256], chan="xt")
        p.emit("act", lambda eng, xv=xv, xb=xb: eng.activation(
            out=x2_v, in_=xv, func=AF.Square),
            reads=[xb], writes=["x2"], label="sq")
        p.mm_acc([(lambda eng, xv=xv, kt=kt: eng.matmul(
            ps[0][0:1, 0:256], sb["onesm"][:, 0:1], xv[:, kt, :],
            start=(kt == 0), stop=(kt == 7)), [xb, "onesm"]) for kt in range(8)],
            "p0a", label="sxmm")
        p.mm_acc([(lambda eng, kt=kt: eng.matmul(
            ps[1][0:1, 0:256], sb["onesm"][:, 0:1], x2_v[:, kt, :],
            start=(kt == 0), stop=(kt == 7)), ["x2", "onesm"]) for kt in range(8)],
            "p1z", label="sx2mm")
        p.emit("act", lambda eng: eng.activation(
            out=sb["mrow"][0:1, :], in_=ps[0][0:1, 0:256], func=AF.Copy, scale=1.0 / D),
            reads=["p0a"], writes=["mrow"], label="ev_mean")
        p.emit("act", lambda eng: eng.activation(
            out=sb["erow"][0:1, :], in_=ps[1][0:1, 0:256], func=AF.Copy, scale=1.0 / D),
            reads=["p1z"], writes=["erow"], label="ev_ex2")
        p.emit("dve", lambda eng: eng.tensor_mul(sb["rrow"][:], sb["mrow"][:], sb["mrow"][:]),
               reads=["mrow"], writes=["rrow"], label="m2")
        p.emit("dve", lambda eng: eng.tensor_sub(sb["rrow"][:], sb["erow"][:], sb["rrow"][:]),
               reads=["erow", "rrow"], writes=["rrow"], label="var")
        p.emit("dve", lambda eng: eng.tensor_scalar_add(
            sb["rrow"][:], sb["rrow"][:], LN_EPS),
            reads=["rrow"], writes=["rrow"], label="vareps")
        p.emit("act", lambda eng: eng.activation(
            out=sb["rrow"][:], in_=sb["rrow"][:], func=AF.Ln),
            reads=["rrow"], writes=["rrow"], label="lnvar")
        p.emit("act", lambda eng: eng.activation(
            out=sb["rrow"][:], in_=sb["rrow"][:], func=AF.Exp, scale=-0.5),
            reads=["rrow"], writes=["rrow"], label="rstd")
        p.emit("pe", lambda eng: eng.matmul(
            ps[2][:, 0:256], sb["onesm"][0:1, :], sb["mrow"][0:1, :],
            start=True, stop=True),
            reads=["mrow", "onesm"], writes=["p2a"], label="meanbc")
        p.emit("pe", lambda eng: eng.matmul(
            ps[3][:, 0:256], sb["onesm"][0:1, :], sb["rrow"][0:1, :],
            start=True, stop=True),
            reads=["rrow", "onesm"], writes=["p3a"], label="rstdbc")
        for kt in range(8):
            p.emit("dve", lambda eng, xv=xv, kt=kt: eng.tensor_sub(
                xn32_v[:, kt, :], xv[:, kt, :], ps[2][:, 0:256]),
                reads=[xb, "p2a"], writes=[f"xn32_{kt}"], label="xsub")
            p.emit("dve", lambda eng, kt=kt: eng.tensor_mul(
                xn32_v[:, kt, :], xn32_v[:, kt, :], ps[3][:, 0:256]),
                reads=[f"xn32_{kt}", "p3a"], writes=[f"xn32_{kt}"], label="xmul")
            p.emit("act", lambda eng, kt=kt, cc=cc: eng.activation(
                out=sb[f"xnc{cc % 2}"][:, kt, :], in_=xn32_v[:, kt, :],
                func=AF.Copy),
                reads=[f"xn32_{kt}"], writes=[f"xnbf{kt}"], label="xnbf")
        p.mm_acc([(lambda eng, kt=kt: eng.matmul(
            ps[4][0:8, 0:256], sb["wgs"][:, kt * 8:(kt + 1) * 8], xn32_v[:, kt, :],
            start=(kt == 0), stop=(kt == 7)),
            [f"xn32_{kt}", "wgs"]) for kt in range(8)], "p4a", label="lgmm")
        p.emit("act", lambda eng, cc=cc: eng.activation(
            out=sb["lgT"][:, (cc % 2) * 256:(cc % 2) * 256 + 256],
            in_=ps[4][0:8, 0:256], func=AF.Copy),
            reads=["p4a"], writes=[f"lgT{cc}"], label="ev_lg")
        p.emit("sp", lambda eng, cc=cc, t0=t0: eng.dma_start(
            out=lgT_o[:, t0:t0 + 256],
            in_=sb["lgT"][:, (cc % 2) * 256:(cc % 2) * 256 + 256]),
            reads=[f"lgT{cc}"], dma=True, chan="st_lg", label="st_lg")
        p.emit("sp", lambda eng, cc=cc: eng.dma_start(
            out=xnT_o[cc], in_=sb[f"xnc{cc % 2}"][:]),
            reads=[f"xnbf{kt}" for kt in range(8)], writes=[f"xnT{cc}"],
            dma=True, chan="st_xn", label="st_xn")

    for m in range(32):
        p.emit("pool", lambda eng, m=m: eng.dma_start(
            out=winB[m], in_=winT[m]),
            writes=[f"wpcx{m}"], dma=True, chan=f"pc{m % 4}", label="pcw")
    for k in range(16):
        p.emit("pool", lambda eng, k=k: eng.dma_start(
            out=woutB[k], in_=woutT[k]),
            writes=[f"wpco{k}"], dma=True, chan=f"pc{k % 4}", label="pco")

    xsl_m = [f"xsl_m{m}" for m in range(16)]
    arena_ln = ["xt0", "xt1", "x2"] + [f"xn32_{kt}" for kt in range(8)]
    p.emit("dve", lambda eng: eng.memset(sb["carry"][:], 0.0), writes=["carry"])
    p.emit("dve", lambda eng: eng.memset(sb["xsl"][:], 0.0),
           writes=["xsl_h"] + xsl_m, label="xsl0")

    def ld_xn(c):
        wb = f"xmc{c % 2}"
        p.emit("sp", lambda eng, c=c, wb=wb: eng.dma_start(
            out=sb[wb][:], in_=xnT_o[c]),
            reads=[f"xnT{c}"], writes=[f"xm{c % 2}"], dma=True,
            chan="xl" + str(c % 2), label="ld_xn")

    def ld_winx(c, et):
        wb = f"winx{et % 2}"
        p.emit("sp", lambda eng, et=et, wb=wb: eng.dma_start(
            out=sb[wb][:], in_=winB[et]),
            reads=[f"wpcx{et}"], writes=[wb], dma=True, chan="wx" + str(et % 2),
            label="ld_wx")

    def ld_winz(c, et):
        wb = f"winz{et % 2}"
        p.emit("sp", lambda eng, et=et, wb=wb: eng.dma_start(
            out=sb[wb][:], in_=winB[16 + et]),
            reads=[f"wpcx{16 + et}"], writes=[wb], dma=True,
            chan="wz" + str(et % 2), label="ld_wz")

    def inx(c, et):
        xm = f"xmc{c % 2}"
        wb = f"winx{et % 2}"
        rb, bk = ("p2a", 2) if et % 2 == 0 else ("p4a", 4)
        p.mm_acc([(lambda eng, kt=kt, bk=bk, wb=wb, xm=xm: eng.matmul(
            ps[bk][:, 0:TC], sb[wb][:, kt, :], sb[xm][:, kt, :],
            start=(kt == 0), stop=(kt == 7)),
            [wb, f"xm{c % 2}"]) for kt in range(8)], rb, label="inx")
        p.emit("act", lambda eng, et=et, bk=bk: eng.activation(
            out=sb["xsl"][:, et, 3:3 + TC], in_=ps[bk][:, 0:TC], func=AF.Copy),
            reads=[rb, "xsl_h"], writes=[xsl_m[et]], label="ev_x")

    def inz(c, k):
        xm = f"xmc{c % 2}"
        wb = f"winz{k % 2}"
        p.mm_acc([(lambda eng, kt=kt, wb=wb, xm=xm: eng.matmul(
            ps[1][:, 0:TC], sb[wb][:, kt, :], sb[xm][:, kt, :],
            start=(kt == 0), stop=(kt == 7)),
            [wb, f"xm{c % 2}"]) for kt in range(8)], "p1z", label="inz")

    def conv_piece(c, k):
        p.emit("act", lambda eng, k=k: eng.activation(
            out=sb["cvt0"][:], in_=sb["xsl"][:, k, 0:TC], func=AF.Copy,
            scale=sb["cwcol"][:, k * 4:k * 4 + 1]),
            reads=[xsl_m[k], "xsl_h", "cwcol"], writes=["cvt0"], label="cv0")
        p.emit("dve", lambda eng, k=k: eng.scalar_tensor_tensor(
            out=sb["cvt1"][:], in0=sb["xsl"][:, k, 1:1 + TC],
            scalar=sb["cwcol"][:, k * 4 + 1:k * 4 + 2],
            in1=sb["cvt0"][:], op0=ALU.mult, op1=ALU.add),
            reads=[xsl_m[k], "xsl_h", "cwcol", "cvt0"], writes=["cvt1"],
            label="cv1")
        p.emit("dve", lambda eng, k=k: eng.scalar_tensor_tensor(
            out=sb["cvt0"][:], in0=sb["xsl"][:, k, 2:2 + TC],
            scalar=sb["cwcol"][:, k * 4 + 2:k * 4 + 3],
            in1=sb["cvt1"][:], op0=ALU.mult, op1=ALU.add),
            reads=[xsl_m[k], "xsl_h", "cwcol", "cvt1"], writes=["cvt0"],
            label="cv2")
        p.emit("dve", lambda eng, k=k: eng.scalar_tensor_tensor(
            out=sb["xcv"][:, k, :], in0=sb["xsl"][:, k, 3:3 + TC],
            scalar=sb["cwcol"][:, k * 4 + 3:k * 4 + 4],
            in1=sb["cvt0"][:], op0=ALU.mult, op1=ALU.add),
            reads=[xsl_m[k], "cwcol", "cvt0"], writes=[f"xcvr{k}"], label="cv3")

    def silupair(c, k):
        p.emit("act", lambda eng, k=k: eng.activation(
            out=sb["zs"][:, k, :], in_=ps[1][:, 0:TC], func=AF.Silu),
            reads=["p1z"], writes=[f"zs{k}"], label="ev_z")
        p.emit("act", lambda eng, k=k: eng.activation(
            out=sb["xcv"][:, k, :], in_=sb["xcv"][:, k, :], func=AF.Silu,
            bias=sb["cbcol"][:, k:k + 1]),
            reads=[f"xcvr{k}", "cbcol"], writes=[f"xcv{k}"], label="silu_cv")

    def xcbxp(c, k):
        xcb = f"xcb{k % 2}"
        p.emit("act", lambda eng, k=k, xcb=xcb: eng.activation(
            out=sb[xcb][:], in_=sb["xcv"][:, k, :], func=AF.Copy),
            reads=[f"xcv{k}"], writes=[xcb], label="xcb")
        for rb, bk, c0, c1 in (("p3a", 3, 0, 64), ("p3b", 6, 64, 112)):
            kw = {}
            if k == 0:
                kw["writes"] = [rb]
            elif k == 15:
                kw["nowait_writes"] = [rb]
            p.emit("pe", lambda eng, k=k, bk=bk, c0=c0, c1=c1, xcb=xcb:
                   eng.matmul(
                       ps[bk][0:c1 - c0, 0:TC],
                       sb["xpTs"][:, k * 112 + c0:k * 112 + c1],
                       sb[xcb][:], start=(k == 0), stop=(k == 15)),
                   reads=[xcb, "xpTs"], label="xp", **kw)

    def xp_evacs(c):
        p.emit("act", lambda eng: eng.activation(
            out=sb["dtin"][:], in_=ps[3][0:64, 0:TC], func=AF.Copy),
            reads=["p3a"], writes=["dtin"], label="ev_dtin")
        p.emit("act", lambda eng: eng.activation(
            out=sb["bcsb"][0:16, 0:256], in_=ps[6][0:16, 0:TC], func=AF.Copy),
            reads=["p3b"], writes=["bcsb_b"], label="ev_B")
        p.emit("act", lambda eng: eng.activation(
            out=sb["bcsb"][0:16, 256:512], in_=ps[6][32:48, 0:TC], func=AF.Copy),
            reads=["p3b"], writes=["bcsb_c"], label="ev_C")

    def sel(c):
        for n in range(16):
            bank, bufs = (2, ["p2a", "p2b"]) if n % 2 == 0 else (4, ["p4a", "p4b"])
            p.emit("pe", lambda eng, n=n, bank=bank: eng.matmul(
                ps[bank][:, 0:512], sb["eall"][:, n * 128:(n + 1) * 128],
                sb["bcsb"][0:16, :], start=True, stop=True),
                reads=["bcsb_b", "bcsb_c", "eall"], writes=bufs, label="selmm")
            wl = [f"bball{n}", f"cball{n}"] + (arena_ln if c == 0 else [])
            p.emit("act", lambda eng, n=n, bank=bank: eng.activation(
                out=ar[:, n * 512:(n + 1) * 512], in_=ps[bank][:, 0:512],
                func=AF.Copy),
                reads=bufs, writes=wl, label="ev_bc")

    def emit_dtmm(c, et):
        c0 = (et % 2) * 256
        pbank = f"p0{'ab'[et % 2]}"
        p.emit("pe", lambda eng, et=et, c0=c0: eng.matmul(
            ps[0][:, c0:c0 + TC], sb["dtTs"][:, et * 128:(et + 1) * 128],
            sb["dtin"][:], start=True, stop=True),
            reads=["dtin", "dtTs"], writes=[pbank], label="dtmm")

    def sp_stage(c, et):
        par = et % 2
        c0 = par * 256
        pbank = f"p0{'ab'[par]}"
        sprb, sptb = f"spr{par}", f"spt{par}"
        p.emit("act", lambda eng, et=et, c0=c0, sprb=sprb: eng.activation(
            out=sb[sprb][:], in_=ps[0][:, c0:c0 + TC], func=AF.Relu,
            bias=sb["dtbcol"][:, et:et + 1]),
            reads=[pbank, "dtbcol"], writes=[sprb], label="sp_r")
        p.emit("act", lambda eng, et=et, c0=c0, sptb=sptb: eng.activation(
            out=sb[sptb][:], in_=ps[0][:, c0:c0 + TC], func=AF.Abs,
            bias=sb["dtbcol"][:, et:et + 1]),
            reads=[pbank, "dtbcol"], writes=[sptb], label="sp_a")
        p.emit("act", lambda eng, sptb=sptb: eng.activation(
            out=sb[sptb][:], in_=sb[sptb][:], func=AF.Exp, scale=-1.0),
            reads=[sptb], writes=[sptb], label="sp_e")
        p.emit("act", lambda eng, sptb=sptb: eng.activation(
            out=sb[sptb][:], in_=sb[sptb][:], func=AF.Ln, bias=1.0),
            reads=[sptb], writes=[sptb], label="sp_l")

    def pool_stage(c, et):
        par = et % 2
        sprb, sptb = f"spr{par}", f"spt{par}"
        dltb, dub = f"dlt{par}", f"du{par}"
        p.emit("pool", lambda eng, dltb=dltb, sptb=sptb, sprb=sprb: eng.tensor_tensor(
            out=sb[dltb][:], in0=sb[sptb][:], in1=sb[sprb][:], op=ALU.add),
            reads=[sptb, sprb], writes=[dltb], label="sp_add")
        p.emit("pool", lambda eng, et=et, dltb=dltb, dub=dub: eng.tensor_tensor(
            out=sb[dub][:], in0=sb[dltb][:], in1=sb["xcv"][:, et, :], op=ALU.mult),
            reads=[dltb, f"xcv{et}"], writes=[dub], label="du")

    def cmul_stage(c, et):
        dub = f"du{et % 2}"
        for half in range(2):
            n0 = half * 8
            p.emit("pool", lambda eng, dub=dub, n0=n0: eng.tensor_tensor(
                out=call_v[:, n0:n0 + 8, :],
                in0=sb[dub][:].unsqueeze(1).broadcast_to([128, 8, TC]),
                in1=bball_v[:, n0:n0 + 8, :], op=ALU.mult),
                reads=[dub] + [f"bball{n}" for n in range(n0, n0 + 8)],
                writes=[f"call{half}"], label="cmul")

    def aexp(c, et, n):
        ab = f"a{n % 2}"
        dltb = f"dlt{et % 2}"
        p.emit("act", lambda eng, et=et, n=n, ab=ab, dltb=dltb: eng.activation(
            out=sb[ab][:], in_=sb[dltb][:], func=AF.Exp,
            scale=sb["aneg"][:, et * 16 + n:et * 16 + n + 1]),
            reads=[dltb, "aneg"], writes=[ab], label="aexp")

    def scan(c, et, n):
        ab = f"a{n % 2}"
        p.emit("dve", lambda eng, et=et, n=n, ab=ab: eng.tensor_tensor_scan(
            out=sb["hall"][:, n, :], data0=sb[ab][:],
            data1=call_v[:, n, :],
            initial=sb["carry"][:, et * 16 + n:et * 16 + n + 1],
            op0=ALU.mult, op1=ALU.add),
            reads=[ab, f"call{n // 8}", "carry"], writes=[f"hall{n}"],
            label="scan")

    def hmul(c, et, half):
        n0 = half * 8
        hhalf = [f"hall{n}" for n in range(n0, n0 + 8)]
        cb_h = [f"cball{n}" for n in range(n0, n0 + 8)]
        p.emit("dve", lambda eng, n0=n0: eng.tensor_tensor(
            out=sb["hbf"][:, n0:n0 + 8, :], in0=sb["hall"][:, n0:n0 + 8, :],
            in1=cball_v[:, n0:n0 + 8, :], op=ALU.mult),
            reads=hhalf + cb_h, writes=[f"hbf{half}"], label="hmul")

    def ymm_y1(c, et):
        hall_all = [f"hall{n}" for n in range(16)]
        p.emit("act", lambda eng, et=et: eng.activation(
            out=sb["carry"][:, et * 16:(et + 1) * 16],
            in_=sb["hall"][:, :, TC - 1:TC].rearrange("p a b -> p (a b)"),
            func=AF.Copy),
            reads=hall_all, writes=["carry"], label="carryw")
        p.mm_acc([(lambda eng, n=n: eng.matmul(
            ps[7][:, 0:TC], sb["ibf"][:], sb["hbf"][:, n, :],
            start=(n == 0), stop=(n == 15)), [f"hbf{n // 8}", "ibf"])
            for n in range(16)], PB[7], label="ymm")
        y1b_ = f"y1{'ab'[et % 2]}"
        p.emit("dve", lambda eng, et=et, y1b_=y1b_: eng.scalar_tensor_tensor(
            out=sb[y1b_][:], in0=sb["xcv"][:, et, :],
            scalar=sb["dcol"][:, et:et + 1], in1=ps[7][:, 0:TC],
            op0=ALU.mult, op1=ALU.add),
            reads=[f"xcv{et}", "dcol", PB[7]], writes=[y1b_], label="y1")

    def yf(c, et):
        y1b_ = f"y1{'ab'[et % 2]}"
        p.emit("pool", lambda eng, et=et, y1b_=y1b_: eng.tensor_tensor(
            out=sb["yfs"][:, et, :], in0=sb[y1b_][:], in1=sb["zs"][:, et, :],
            op=ALU.mult),
            reads=[y1b_, f"zs{et}"], writes=[f"yfs{et}"], label="yf")

    def ld_wout(k):
        wb = f"wout{k % 4}"
        p.emit("sp", lambda eng, k=k, wb=wb: eng.dma_start(
            out=sb[wb][:], in_=woutB[k]),
            reads=[f"wpco{k}"], writes=[wb], dma=True, chan="wo" + str(k % 4),
            label="ld_wo")

    st_toks = []

    def out_phase(co):
        t0 = co * TC
        GROUPS = ((0, 0, 5, "p5"), (0, 1, 3, "p3a"), (1, 0, 6, "p3b"),
                  (1, 1, 1, "p1z"))
        for k in range(16):
            ld_wout(k)
            wb = f"wout{k % 4}"
            for j, dch, bank, tok in GROUPS:
                kw = {}
                if k == 0:
                    kw["writes"] = [tok]
                elif k == 15:
                    kw["nowait_writes"] = [tok]
                p.emit("pe", lambda eng, k=k, j=j, bank=bank, wb=wb, dch=dch:
                       eng.matmul(
                           ps[bank][:, :],
                           sb["yfs"][:, k, j * 128:(j + 1) * 128],
                           sb[wb][:, dch * 512:(dch + 1) * 512],
                           start=(k == 0), stop=(k == 15)),
                       reads=[f"yfs{k}", wb], label="outp", **kw)
        for j, pair in ((0, ((5, "p5"), (3, "p3a"))),
                        (1, ((6, "p3b"), (1, "p1z")))):
            for half, (bank, tok) in enumerate(pair):
                p.emit("act", lambda eng, bank=bank, half=half: eng.activation(
                    out=sb["sevac"][:, half * 512:half * 512 + 512],
                    in_=ps[bank][:, :], func=AF.Copy),
                    reads=[tok], writes=[f"sev{half}"], label="ev_s")
            tok2 = p.emit("sp", lambda eng, t0=t0, j=j: eng.dma_start(
                out=s_o[t0 + j * 128:t0 + (j + 1) * 128, :], in_=sb["sevac"][:]),
                reads=["sev0", "sev1"], dma=True, chan="st_s", label="st_s")
            st_toks.append(tok2)

    def iteration(c, et, last):
        n1 = c + 1
        if 2 <= et and et + 2 <= 15:
            emit_dtmm(c, et + 2)
        if not last:
            if et + 1 <= 15:
                ld_winx(n1, et + 1)
            ld_winz(n1, et)
            inx(n1, et)
            if et >= 1:
                inz(n1, et - 1)
            if et >= 2:
                xcbxp(n1, et - 2)
        if et >= 1:
            yf(c, et - 1)
        if not last and et >= 1:
            conv_piece(n1, et - 1)
        aexp(c, et, 0)
        aexp(c, et, 1)
        for n in range(16):
            scan(c, et, n)
            if n == 7:
                hmul(c, et, 0)
            if n + 2 < 16:
                aexp(c, et, n + 2)
            if n == 2 and not last:
                pass
            if n == 3 and et + 1 <= 15:
                sp_stage(c, et + 1)
            if n == 6 and not last and et >= 1:
                silupair(n1, et - 1)
        hmul(c, et, 1)
        ymm_y1(c, et)
        if et == 0:
            emit_dtmm(c, 3)
        if et + 1 <= 15:
            pool_stage(c, et + 1)
            cmul_stage(c, et + 1)

    def post_loop(c):
        n1 = c + 1
        yf(c, 15)
        inz(n1, 15)
        conv_piece(n1, 15)
        silupair(n1, 15)
        xcbxp(n1, 14)
        xcbxp(n1, 15)
        xp_evacs(n1)
        sel(n1)
        emit_dtmm(n1, 0)
        emit_dtmm(n1, 1)
        sp_stage(n1, 0)
        pool_stage(n1, 0)
        cmul_stage(n1, 0)
        emit_dtmm(n1, 2)
        if n1 + 1 < NCH:
            ld_winx(n1 + 1, 0)
            ld_xn(n1 + 1)

    ld_xn(0)
    for et in range(16):
        ld_winx(0, et)
        inx(0, et)
    for k in range(16):
        ld_winz(0, k)
        conv_piece(0, k)
        inz(0, k)
        silupair(0, k)
    for k in range(16):
        xcbxp(0, k)
    xp_evacs(0)
    sel(0)
    emit_dtmm(0, 0)
    emit_dtmm(0, 1)
    sp_stage(0, 0)
    pool_stage(0, 0)
    cmul_stage(0, 0)
    emit_dtmm(0, 2)
    ld_winx(1, 0)
    ld_xn(1)

    for c in range(NCH):
        last = (c + 1 >= NCH)
        if c >= 1:
            out_phase(c - 1)
        if not last:
            p.emit("dve", lambda eng: eng.tensor_copy(
                sb["xsl"][:, :, 0:3], sb["xsl"][:, :, TC:TC + 3]),
                reads=xsl_m, writes=["xsl_h"], label="hist")
        for et in range(16):
            iteration(c, et, last)
        if not last:
            post_loop(c)
        else:
            yf(c, 15)
    out_phase(NCH - 1)
    p.emit("act", lambda eng: eng.activation(
        out=sb["mrow"][0:1, :], in_=sb["mrow"][0:1, :], func=AF.Copy),
        reads=["mrow"], deps=list(st_toks[-4:]), label="drain")

    with nc.Block() as block:
        p.build(block)
    es.close()
    return nc


def _xpt_pad(xpw):
    a = np.ascontiguousarray(xpw.T).reshape(16, 128, 96).astype(np.float32)
    out = np.zeros((16, 128, 112), np.float32)
    out[:, :, 0:64] = a[:, :, 0:64]
    out[:, :, 64:80] = a[:, :, 64:80]
    out[:, :, 96:112] = a[:, :, 80:96]
    return out


def _l1_inputs(x_b, ipw, cw, cb, xpw, dtw, dtb, alog, dvec, opw, wg_):
    f32 = np.float32
    onesm, identf, ibf, eall = _consts_np()
    return {
        "xT": np.ascontiguousarray(x_b.T).reshape(8, 128, L).astype(f32),
        "winT": np.ascontiguousarray(
            ipw.T.reshape(8, 128, 32, 128).transpose(2, 1, 0, 3)).astype(f32),
        "woutT": np.ascontiguousarray(opw.T).reshape(16, 128, D).astype(f32),
        "xpT": _xpt_pad(xpw),
        "dtT": np.ascontiguousarray(dtw.T).astype(f32),
        "wg": np.ascontiguousarray(wg_).reshape(8, 128, 8).astype(f32),
        "acol": np.ascontiguousarray(
            alog.reshape(16, 128, 16).transpose(1, 0, 2).reshape(128, 256)).astype(f32),
        "dcol": np.ascontiguousarray(dvec.reshape(16, 128).T).astype(f32),
        "cbcol": np.ascontiguousarray(cb.reshape(16, 128).T).astype(f32),
        "dtbcol": np.ascontiguousarray(dtb.reshape(16, 128).T).astype(f32),
        "cwcol": np.ascontiguousarray(
            cw.reshape(16, 128, 4).transpose(1, 0, 2).reshape(128, 64)).astype(f32),
        "onesm": onesm, "eall": eall, "ibf": ibf,
    }


TRASH = 100000.0


def build_l2():
    nc = bass.Bass("TRN2", target_bir_lowering=False, debug=False)
    lg_i = nc.dram_tensor("lg", [128, 128, 8], F32, kind="ExternalInput").ap()
    xnf_i = nc.dram_tensor("xnf", [T, D], BF16, kind="ExternalInput").ap()
    wgt_i = nc.dram_tensor("wgt", [8, 128, FF], F32, kind="ExternalInput").ap()
    wdn_i = nc.dram_tensor("wdn", [32, 128, D], F32, kind="ExternalInput").ap()
    onesm_i = nc.dram_tensor("onesm", [128, 128], F32, kind="ExternalInput").ap()
    identf_i = nc.dram_tensor("identf", [128, 128], F32, kind="ExternalInput").ap()
    ibf_i = nc.dram_tensor("ibf", [128, 128], BF16, kind="ExternalInput").ap()
    iota8_i = nc.dram_tensor("iota8", [128, 8], F32, kind="ExternalInput").ap()
    c8b_i = nc.dram_tensor("c8b", [128, 8], F32, kind="ExternalInput").ap()
    ccol_i = nc.dram_tensor("ccol", [128, 4], F32, kind="ExternalInput").ap()
    io128_i = nc.dram_tensor("io128", [128, 128], F32, kind="ExternalInput").ap()
    io32_i = nc.dram_tensor("io32", [128, 32], F32, kind="ExternalInput").ap()
    thr32_i = nc.dram_tensor("thr32", [128, 32], F32, kind="ExternalInput").ap()
    iotac_i = nc.dram_tensor("iotac", [128, 1], F32, kind="ExternalInput").ap()

    eo_o = nc.dram_tensor("eo_o", [CAP, D], F32, kind="ExternalOutput").ap()
    rt_o = nc.dram_tensor("rt_o", [128, 4, 128], F32, kind="ExternalOutput").ap()
    wgt_bf = nc.dram_tensor("wgt_bf", [8, 128, FF], BF16).ap()
    wdn_bf = nc.dram_tensor("wdn_bf", [32, 128, D], BF16).ap()

    es = ExitStack()
    sb = {}
    for name, shape, dt in [
        ("lg", [128, 1024], F32),
        ("onesm", [128, 128], F32),
        ("eall", [16, 16 * 128], F32), ("identf", [128, 128], F32),
        ("ibf", [128, 128], BF16),
        ("iota8", [128, 8], F32), ("c8b", [128, 8], F32), ("ccol", [128, 4], F32),
        ("io128", [128, 128], F32), ("io32", [128, 32], F32),
        ("thr32", [128, 32], F32), ("iotac", [128, 1], F32),
        ("m1", [128, 128], F32), ("m2", [128, 128], F32),
        ("e1", [128, 128], F32), ("e2", [128, 128], F32),
        ("mk1", [128, 1024], F32), ("mk2", [128, 1024], F32),
        ("lno", [128, 1024], F32),
        ("cnd", [128, 1024], F32),
        ("cum1", [128, 1024], F32), ("cum2", [128, 1024], F32),
        ("tot", [128, 8], F32), ("c1t", [8, 1], F32),
        ("t1T", [8, 128], F32), ("t2T", [8, 128], F32),
        ("o1T", [8, 128], F32), ("o2T", [8, 128], F32),
        ("off1", [128, 8], F32), ("off2", [128, 8], F32),
        ("g1", [128, 128], F32), ("g2", [128, 128], F32),
        ("l1v", [128, 128], F32), ("l2v", [128, 128], F32),
        ("k1", [128, 128], F32), ("k2", [128, 128], F32),
        ("f1g", [128, 128], F32), ("f2g", [128, 128], F32),
        ("d12", [128, 128], F32),
        ("cmy", [128, 128], F32), ("cmy2", [128, 128], F32),
        ("clo", [128, 128], F32), ("chi", [128, 128], F32),
        ("chiT", [128, 128], F32), ("cloT", [128, 128], F32),
        ("big3", [128, 4096], F32),
        ("ihi", [128, 128], BF16), ("ilo", [128, 32], BF16),
        ("vh", [128, 32], BF16), ("vl", [128, 32], BF16),
        ("tabh", [128, 32], F32), ("tabi", [128, 32], F32),
        ("idxt", [128, 32], I32),
        ("bigt", [128, 128], F32),
        ("dsp0", [128, D], BF16), ("dsp1", [128, D], BF16),
        ("dspT", [128, 8, CAP], BF16),
        ("wga", [128, 8, 128], BF16), ("wgb", [128, 8, 128], BF16),
        ("wda", [128, D], BF16), ("wdb", [128, D], BF16),
        ("hbf", [128, 32, 512], BF16),
        ("eoev", [128, 4, D], F32),
    ]:
        sb[name] = es.enter_context(nc.sbuf_tensor("s_" + name, shape, dt))

    ps = [es.enter_context(nc.psum_tensor(f"ps{i}", [128, 512], F32))
          for i in range(8)]
    PB = [f"bank{i}" for i in range(8)]

    p = Prog(nc, es)

    def load(name, dst_ap, src_ap, cast=False, chan=None, wbuf=None, deps=()):
        eng = "pool" if cast else "sp"
        return p.emit(eng, lambda eng_, d=dst_ap, s=src_ap: eng_.dma_start(out=d, in_=s),
                      writes=[wbuf or name], dma=True, chan=chan, deps=deps,
                      label="ld_" + name)

    for nm, srcap in [("lg", lg_i.rearrange("p f e -> p f e")), ("onesm", onesm_i),
                      ("identf", identf_i), ("ibf", ibf_i), ("iota8", iota8_i),
                      ("c8b", c8b_i), ("ccol", ccol_i), ("io128", io128_i),
                      ("io32", io32_i), ("thr32", thr32_i), ("iotac", iotac_i)]:
        dst = sb[nm][:]
        if nm == "lg":
            dst = sb["lg"][:].rearrange("p (f e) -> p f e", e=8)
        load(nm, dst, srcap)
    p.emit("dve", lambda eng: eng.memset(sb["bigt"][:], BIG), writes=["bigt"])
    for i in range(4):
        p.emit("pool", lambda eng, i=i: eng.dma_start(
            out=wgt_bf[2 * i:2 * i + 2], in_=wgt_i[2 * i:2 * i + 2]),
            writes=[f"wgtbf{i}"], dma=True, chan=f"wc{i % 2}", label="cast_wg")
        p.emit("pool", lambda eng, i=i: eng.dma_start(
            out=wdn_bf[8 * i:8 * i + 8], in_=wdn_i[8 * i:8 * i + 8]),
            writes=[f"wdnbf{i}"], dma=True, chan=f"wc{i % 2}", label="cast_wd")

    lg_v = sb["lg"][:].rearrange("p (f e) -> p f e", e=8)

    def emaj(t):
        return sb[t][:].rearrange("p (e f) -> p e f", e=8).transpose([0, 2, 1])
    mk1_pfe = emaj("mk1"); mk2_pfe = emaj("mk2")
    lno_pfe = emaj("lno"); cnd_pfe = emaj("cnd")
    iota_b = sb["iota8"][:].unsqueeze(1).broadcast_to([128, 128, 8])
    c8b_b = sb["c8b"][:].unsqueeze(1).broadcast_to([128, 128, 8])

    p.emit("dve", lambda eng: eng.reduce_max(sb["m1"][:], lg_v, axis=AX.X),
           reads=["lg"], writes=["m1"], label="m1")
    p.emit("dve", lambda eng: eng.tensor_tensor(
        out=mk1_pfe, in0=lg_v, in1=sb["m1"][:].unsqueeze(2).broadcast_to([128, 128, 8]),
        op=ALU.is_equal), reads=["lg", "m1"], writes=["mk1"], label="mk1raw")
    p.emit("dve", lambda eng: eng.scalar_tensor_tensor(
        out=cnd_pfe, in0=mk1_pfe, scalar=-BIG, in1=c8b_b,
        op0=ALU.mult, op1=ALU.add),
        reads=["mk1", "c8b"], writes=["cnd"], label="cnd1")
    p.emit("dve", lambda eng: eng.tensor_reduce(
        sb["e1"][:], cnd_pfe, axis=AX.X, op=ALU.min),
        reads=["cnd"], writes=["e1"], label="e1")
    p.emit("dve", lambda eng: eng.tensor_tensor(
        out=mk1_pfe, in0=iota_b, in1=sb["e1"][:].unsqueeze(2).broadcast_to([128, 128, 8]),
        op=ALU.is_equal), reads=["iota8", "e1", "mk1"], writes=["mk1"], label="mk1")
    p.emit("dve", lambda eng: eng.scalar_tensor_tensor(
        out=lno_pfe, in0=mk1_pfe, scalar=-BIG, in1=lg_v,
        op0=ALU.mult, op1=ALU.add),
        reads=["mk1", "lg"], writes=["lno"], label="lno")
    p.emit("dve", lambda eng: eng.reduce_max(sb["m2"][:], lno_pfe, axis=AX.X),
           reads=["lno"], writes=["m2"], label="m2")
    p.emit("dve", lambda eng: eng.tensor_tensor(
        out=cnd_pfe, in0=lno_pfe, in1=sb["m2"][:].unsqueeze(2).broadcast_to([128, 128, 8]),
        op=ALU.is_equal), reads=["lno", "m2", "cnd"], writes=["cnd"], label="mk2raw")
    p.emit("dve", lambda eng: eng.scalar_tensor_tensor(
        out=cnd_pfe, in0=cnd_pfe, scalar=-BIG, in1=c8b_b,
        op0=ALU.mult, op1=ALU.add),
        reads=["cnd", "c8b"], writes=["cnd"], label="cnd2")
    p.emit("dve", lambda eng: eng.tensor_reduce(
        sb["e2"][:], cnd_pfe, axis=AX.X, op=ALU.min),
        reads=["cnd"], writes=["e2"], label="e2")
    p.emit("dve", lambda eng: eng.tensor_tensor(
        out=mk2_pfe, in0=iota_b, in1=sb["e2"][:].unsqueeze(2).broadcast_to([128, 128, 8]),
        op=ALU.is_equal), reads=["iota8", "e2"], writes=["mk2"], label="mk2")
    p.emit("dve", lambda eng: eng.tensor_sub(sb["d12"][:], sb["m1"][:], sb["m2"][:]),
           reads=["m1", "m2"], writes=["d12"], label="d12")
    p.emit("act", lambda eng: eng.activation(
        out=sb["g1"][:], in_=sb["d12"][:], func=AF.Sigmoid),
        reads=["d12"], writes=["g1"], label="g1")
    p.emit("act", lambda eng: eng.activation(
        out=sb["g2"][:], in_=sb["d12"][:], func=AF.Sigmoid, scale=-1.0),
        reads=["d12"], writes=["g2"], label="g2")

    for mk, cum in (("mk1", "cum1"), ("mk2", "cum2")):
        for e in range(8):
            p.emit("dve", lambda eng, mk=mk, cum=cum, e=e: eng.tensor_tensor_scan(
                out=sb[cum][:, e * 128:(e + 1) * 128],
                data0=sb["onesm"][:],
                data1=sb[mk][:, e * 128:(e + 1) * 128],
                initial=0.0, op0=ALU.mult, op1=ALU.add),
                reads=[mk, "onesm"], writes=[f"{cum}_{e}"], label="cum")
    cum1_all = [f"cum1_{e}" for e in range(8)]
    cum2_all = [f"cum2_{e}" for e in range(8)]

    def cross_p(cum_all, cum, totb, tT, oT, off, extra=None):
        p.emit("dve", lambda eng: eng.tensor_copy(
            sb["tot"][:], sb[cum][:].rearrange("p (e f) -> p e f", e=8)[:, :, 127]),
            reads=cum_all, writes=[totb], label="tot")
        p.emit("pe", lambda eng: eng.transpose(
            ps[0][0:8, 0:128], sb["tot"][:], sb["identf"][:]),
            reads=[totb, "identf"], writes=[PB[0]], label="tp_tot")
        p.emit("act", lambda eng: eng.activation(
            out=sb[tT][:], in_=ps[0][0:8, 0:128], func=AF.Copy),
            reads=[PB[0]], writes=[tT], label="ev_tT")
        p.emit("dve", lambda eng: eng.tensor_tensor_scan(
            out=sb[oT][:], data0=sb["onesm"][0:8, :], data1=sb[tT][:],
            initial=0.0, op0=ALU.mult, op1=ALU.add),
            reads=[tT, "onesm"], writes=[oT], label="pscan")
        p.emit("dve", lambda eng: eng.tensor_sub(sb[oT][:], sb[oT][:], sb[tT][:]),
               reads=[oT, tT], writes=[oT], label="excl")
        if extra is not None:
            p.emit("dve", lambda eng: eng.tensor_tensor(
                out=sb[oT][:], in0=sb[oT][:], in1=extra, op=ALU.add),
                reads=[oT, "c1t"], writes=[oT], label="addc1")
        p.emit("pe", lambda eng: eng.transpose(
            ps[0][:, 0:8], sb[oT][:], sb["identf"][0:8, 0:8]),
            reads=[oT, "identf"], writes=[PB[0]], label="tp_off")
        p.emit("act", lambda eng: eng.activation(
            out=sb[off][:], in_=ps[0][:, 0:8], func=AF.Copy),
            reads=[PB[0]], writes=[off], label="ev_off")

    cross_p(cum1_all, "cum1", "tot1b", "t1T", "o1T", "off1")
    p.emit("dve", lambda eng: eng.tensor_add(
        sb["c1t"][:], sb["o1T"][:, 127:128], sb["t1T"][:, 127:128]),
        reads=["o1T", "t1T"], writes=["c1t"], label="c1tot")
    cross_p(cum2_all, "cum2", "tot2b", "t2T", "o2T", "off2",
            extra=sb["c1t"][:].broadcast_to([8, 128]))

    def loc_pipeline(mk, cum, off, lv):
        cum_all = [f"{cum}_{e}" for e in range(8)]
        mk_v = sb[mk][:].rearrange("p (e f) -> p e f", e=8)
        cum_v = sb[cum][:].rearrange("p (e f) -> p e f", e=8)
        p.emit("dve", lambda eng: eng.scalar_tensor_tensor(
            out=cum_v, in0=mk_v, scalar=-1.0, in1=cum_v,
            op0=ALU.mult, op1=ALU.add),
            reads=[mk] + cum_all, writes=cum_all, label="excl2")
        p.emit("dve", lambda eng: eng.tensor_tensor(
            out=cum_v, in0=cum_v,
            in1=sb[off][:].transpose([0, 1]).unsqueeze(2).broadcast_to([128, 8, 128]),
            op=ALU.add),
            reads=cum_all + [off], writes=cum_all, label="addoff")
        p.emit("dve", lambda eng: eng.tensor_tensor(
            out=cum_v, in0=cum_v, in1=mk_v, op=ALU.mult),
            reads=cum_all + [mk], writes=cum_all, label="selmk")
        p.emit("dve", lambda eng: eng.tensor_reduce(
            sb[lv][:], cum_v.transpose([0, 2, 1]), axis=AX.X, op=ALU.add),
            reads=cum_all, writes=[lv], label="locsum")

    loc_pipeline("mk1", "cum1", "off1", "l1v")
    loc_pipeline("mk2", "cum2", "off2", "l2v")

    for lv, ev, kv, fg, gv in (("l1v", "e1", "k1", "f1g", "g1"),
                               ("l2v", "e2", "k2", "f2g", "g2")):
        p.emit("dve", lambda eng, lv=lv, kv=kv: eng.tensor_single_scalar(
            sb[kv][:], sb[lv][:], float(CAP), ALU.is_lt),
            reads=[lv], writes=[kv], label="keep")
        p.emit("dve", lambda eng, lv=lv: eng.tensor_scalar_min(
            sb[lv][:], sb[lv][:], float(CAP - 1)),
            reads=[lv], writes=[lv], label="clamp")
        p.emit("dve", lambda eng, lv=lv, ev=ev, fg=fg: eng.scalar_tensor_tensor(
            out=sb[fg][:], in0=sb[ev][:], scalar=float(CAP), in1=sb[lv][:],
            op0=ALU.mult, op1=ALU.add),
            reads=[ev, lv], writes=[fg], label="fslot")
        p.emit("dve", lambda eng, fg=fg, kv=kv: eng.tensor_mul(
            sb[fg][:], sb[fg][:], sb[kv][:]),
            reads=[fg, kv], writes=[fg], label="fkeep")
        p.emit("dve", lambda eng, gv=gv, kv=kv: eng.tensor_mul(
            sb[gv][:], sb[gv][:], sb[kv][:]),
            reads=[gv, kv], writes=[gv], label="gk")

    for i, nm in enumerate(("f1g", "f2g", "g1", "g2")):
        p.emit("sp", lambda eng, i=i, nm=nm: eng.dma_start(
            out=rt_o[:, i, :], in_=sb[nm][:]),
            reads=[nm], dma=True, chan="rt", label="st_rt")

    ecol = sb["ccol"][:, 2:3]
    for ev, lv, kv, dst in (("e1", "l1v", "k1", "cmy"), ("e2", "l2v", "k2", "cmy2")):
        p.emit("dve", lambda eng, ev=ev: eng.tensor_scalar(
            out=sb["clo"][:], in0=sb[ev][:], scalar1=ecol, scalar2=None,
            op0=ALU.is_equal),
            reads=[ev, "ccol"], writes=["minetmp"], label="mine")
        p.emit("dve", lambda eng, kv=kv: eng.tensor_mul(
            sb["clo"][:], sb["clo"][:], sb[kv][:]),
            reads=["minetmp", kv], writes=["minetmp"], label="minek")
        p.emit("dve", lambda eng, lv=lv, dst=dst: eng.tensor_scalar_add(
            sb[dst][:], sb[lv][:], -BIG),
            reads=[lv], writes=[dst], label="cmysub")
        p.emit("dve", lambda eng, dst=dst: eng.tensor_mul(
            sb[dst][:], sb[dst][:], sb["clo"][:]),
            reads=[dst, "minetmp"], writes=[dst], label="cmymul")
        p.emit("dve", lambda eng, dst=dst: eng.tensor_scalar_add(
            sb[dst][:], sb[dst][:], BIG),
            reads=[dst], writes=[dst], label="cmyadd")
    p.emit("dve", lambda eng: eng.tensor_tensor(
        out=sb["cmy"][:], in0=sb["cmy"][:], in1=sb["cmy2"][:], op=ALU.min),
        reads=["cmy", "cmy2"], writes=["cmy"], label="cmymin")

    p.emit("dve", lambda eng: eng.tensor_tensor(
        out=sb["big3"][:].rearrange("p (f l) -> p f l", l=32),
        in0=sb["cmy"][:].unsqueeze(2).broadcast_to([128, 128, 32]),
        in1=sb["thr32"][:].unsqueeze(1).broadcast_to([128, 128, 32]),
        op=ALU.is_ge),
        reads=["cmy", "thr32"], writes=["big3b"], label="clocmp")
    p.emit("dve", lambda eng: eng.tensor_reduce(
        sb["clo"][:], sb["big3"][:].rearrange("p (f l) -> p f l", l=32),
        axis=AX.X, op=ALU.add),
        reads=["big3b"], writes=["clo"], label="closum")
    p.emit("dve", lambda eng: eng.scalar_tensor_tensor(
        out=sb["chi"][:], in0=sb["clo"][:], scalar=-128.0, in1=sb["cmy"][:],
        op0=ALU.mult, op1=ALU.add),
        reads=["clo", "cmy"], writes=["chi"], label="chi")
    p.emit("pe", lambda eng: eng.transpose(
        ps[0][:, 0:128], sb["chi"][:], sb["identf"][:]),
        reads=["chi", "identf"], writes=[PB[0]], label="tp_chi")
    p.emit("act", lambda eng: eng.activation(
        out=sb["chiT"][:], in_=ps[0][:, 0:128], func=AF.Copy),
        reads=[PB[0]], writes=["chiT"], label="ev_chiT")
    p.emit("pe", lambda eng: eng.transpose(
        ps[0][:, 0:128], sb["clo"][:], sb["identf"][:]),
        reads=["clo", "identf"], writes=[PB[0]], label="tp_clo")
    p.emit("act", lambda eng: eng.activation(
        out=sb["cloT"][:], in_=ps[0][:, 0:128], func=AF.Copy),
        reads=[PB[0]], writes=["cloT"], label="ev_cloT")

    for kt in range(128):
        p.emit("dve", lambda eng, kt=kt: eng.tensor_tensor(
            out=sb["ihi"][:], in0=sb["chiT"][:, kt:kt + 1].broadcast_to([128, 128]),
            in1=sb["io128"][:], op=ALU.is_equal),
            reads=["chiT", "io128"], writes=["ihi"], label="ihi")
        p.emit("dve", lambda eng, kt=kt: eng.tensor_tensor(
            out=sb["ilo"][:], in0=sb["cloT"][:, kt:kt + 1].broadcast_to([128, 32]),
            in1=sb["io32"][:], op=ALU.is_equal),
            reads=["cloT", "io32"], writes=["ilo"], label="ilo")
        p.emit("dve", lambda eng, kt=kt: eng.tensor_scalar_mul(
            sb["vh"][:], sb["ilo"][:], float(kt)),
            reads=["ilo"], writes=["vh"], label="vh")
        p.emit("dve", lambda eng: eng.tensor_scalar_mul(
            sb["vl"][:], sb["ilo"][:], sb["iotac"][:, 0:1]),
            reads=["ilo", "iotac"], writes=["vl"], label="vl")
        for bi, rhs, rb in ((1, "vh", "vh"), (2, "vl", "vl"), (3, "ilo", "ilo")):
            kw = {}
            if kt == 0:
                kw["writes"] = [PB[bi]]
            elif kt == 127:
                kw["nowait_writes"] = [PB[bi]]
            p.emit("pe", lambda eng, kt=kt, bi=bi, rhs=rhs: eng.matmul(
                ps[bi][:, 0:32], sb["ihi"][:], sb[rhs][:],
                start=(kt == 0), stop=(kt == 127)),
                reads=["ihi", rb], label="tabmm", **kw)
    p.emit("act", lambda eng: eng.activation(
        out=sb["tabh"][:], in_=ps[1][:, 0:32], func=AF.Copy),
        reads=[PB[1]], writes=["tabh"], label="ev_tabh")
    p.emit("dve", lambda eng: eng.scalar_tensor_tensor(
        out=sb["tabh"][:], in0=sb["tabh"][:], scalar=128.0, in1=ps[2][:, 0:32],
        op0=ALU.mult, op1=ALU.add),
        reads=["tabh", PB[2]], writes=["tabh"], label="tabcomb")
    p.emit("act", lambda eng: eng.activation(
        out=sb["tabi"][:], in_=ps[3][:, 0:32], func=AF.Copy),
        reads=[PB[3]], writes=["tabi"], label="ev_fill")
    p.emit("dve", lambda eng: eng.tensor_scalar_add(
        sb["tabh"][:], sb["tabh"][:], -BIG),
        reads=["tabh"], writes=["tabh"], label="tabsub")
    p.emit("dve", lambda eng: eng.tensor_mul(
        sb["tabh"][:], sb["tabh"][:], sb["tabi"][:]),
        reads=["tabh", "tabi"], writes=["tabh"], label="tabmul")
    p.emit("dve", lambda eng: eng.tensor_scalar_add(
        sb["tabh"][:], sb["tabh"][:], BIG),
        reads=["tabh"], writes=["tabh"], label="tabadd")
    p.emit("dve", lambda eng: eng.tensor_copy(sb["idxt"][:], sb["tabh"][:]),
           reads=["tabh"], writes=["idxt"], label="idxint")

    for cc in range(32):
        db = f"dsp{cc % 2}"
        p.emit("dve", lambda eng, db=db: eng.memset(sb[db][:], 0.0),
               writes=[db], label="dsp0")
        p.emit("pool", lambda eng, db=db, cc=cc: eng.indirect_dma_start(
            out=sb[db][:, :], out_offset=None, in_=xnf_i,
            in_offset=bass.IndirectOffsetOnAxis(sb["idxt"][:, cc:cc + 1], 0),
            bounds_check=T - 1, oob_is_err=False),
            reads=["idxt", db], writes=[db], dma=True, chan=f"g{cc % 2}",
            label="dgather")
        for kt in range(8):
            p.emit("pe", lambda eng, db=db, kt=kt: eng.transpose(
                ps[4][:, 0:128].bitcast(BF16)[:, 0:128],
                sb[db][:, kt * 128:(kt + 1) * 128], sb["ibf"][:]),
                reads=[db, "ibf"], writes=[PB[4]], label="tp_dsp")
            p.emit("act", lambda eng, cc=cc, kt=kt: eng.activation(
                out=sb["dspT"][:, kt, cc * 128:(cc + 1) * 128],
                in_=ps[4][:, 0:128].bitcast(BF16)[:, 0:128], func=AF.Copy),
                reads=[PB[4]], writes=[f"dspT{kt}"], label="ev_dspT")
    dspT_all = [f"dspT{kt}" for kt in range(8)]

    for tc8 in range(8):
        tt0 = tc8 * 512
        for fs in range(32):
            wb = "wga" if fs % 2 == 0 else "wgb"
            p.emit("sp", lambda eng, wb=wb, fs=fs: eng.dma_start(
                out=sb[wb][:],
                in_=wgt_bf.rearrange("k p f -> p k f")[:, :, fs * 128:(fs + 1) * 128]),
                reads=[f"wgtbf{i}" for i in range(4)], writes=[wb], dma=True,
                chan=wb, label="ld_wga")
            bank, pb = (ps[5], PB[5]) if fs % 2 == 0 else (ps[6], PB[6])
            p.mm_acc([(lambda eng, wb=wb, kt=kt, tt0=tt0, bank=bank: eng.matmul(
                bank[:, :], sb[wb][:, kt, :], sb["dspT"][:, kt, tt0:tt0 + 512],
                start=(kt == 0), stop=(kt == 7)),
                [wb] + dspT_all) for kt in range(8)], pb, label="gate")
            p.emit("act", lambda eng, fs=fs, bank=bank, pb=pb: eng.activation(
                out=sb["hbf"][:, fs, :], in_=bank[:, :], func=AF.Gelu),
                reads=[pb], writes=[f"hbf{fs}"], label="gelu")
        for fs in range(32):
            wb = "wda" if fs % 2 == 0 else "wdb"
            p.emit("sp", lambda eng, wb=wb, fs=fs: eng.dma_start(
                out=sb[wb][:],
                in_=wdn_bf[fs:fs + 1].rearrange("a p d -> p a d").squeeze(1)),
                reads=[f"wdnbf{i}" for i in range(4)], writes=[wb], dma=True,
                chan=wb, label="ld_wda")
            for j in range(4):
                for dch in range(2):
                    bi = 4 * dch + j
                    if bi >= 8:
                        continue
                    kw = {}
                    if fs == 0:
                        kw["writes"] = [PB[bi]]
                    elif fs == 31:
                        kw["nowait_writes"] = [PB[bi]]
                    p.emit("pe", lambda eng, fs=fs, j=j, dch=dch, wb=wb, bi=bi:
                           eng.matmul(
                               ps[bi][:, :],
                               sb["hbf"][:, fs, j * 128:(j + 1) * 128],
                               sb[wb][:, dch * 512:(dch + 1) * 512],
                               start=(fs == 0), stop=(fs == 31)),
                           reads=[f"hbf{fs}", wb], label="down", **kw)
        for j in range(4):
            for dch in range(2):
                bi = 4 * dch + j
                p.emit("act", lambda eng, j=j, dch=dch, bi=bi: eng.activation(
                    out=sb["eoev"][:, j, dch * 512:(dch + 1) * 512],
                    in_=ps[bi][:, :], func=AF.Copy),
                    reads=[PB[bi]], writes=[f"eoev{j}"], label="ev_eo")
        for j in range(4):
            p.emit("sp", lambda eng, tt0=tt0, j=j: eng.dma_start(
                out=eo_o[tt0 + j * 128:tt0 + (j + 1) * 128, :],
                in_=sb["eoev"][:, j, :]),
                reads=[f"eoev{j}"], dma=True, chan="st_eo", label="st_eo")

    with nc.Block() as block:
        p.build(block)
    es.close()
    return nc


def _l2_inputs(e, lg_full, xn_full, w_gate_e, w_down_e):
    f32 = np.float32
    onesm, identf, ibf, eall = _consts_np()
    iota8 = np.tile(np.arange(8, dtype=f32), (128, 1))
    c8b = iota8 + BIG
    ccol = np.zeros((128, 4), f32)
    ccol[:, 0] = e * CAP
    ccol[:, 1] = (e + 1) * CAP
    ccol[:, 2] = e
    io128 = np.tile(np.arange(128, dtype=f32), (128, 1))
    io32 = np.tile(np.arange(32, dtype=f32), (128, 1))
    thr32 = np.tile(128.0 * np.arange(1, 33, dtype=f32), (128, 1))
    iotac = np.arange(128, dtype=f32).reshape(128, 1)
    return {
        "lg": np.ascontiguousarray(lg_full.reshape(128, 128, 8)).astype(f32),
        "xnf": xn_full,
        "wgt": np.ascontiguousarray(w_gate_e).reshape(8, 128, FF).astype(f32),
        "wdn": np.ascontiguousarray(w_down_e).reshape(32, 128, D).astype(f32),
        "onesm": onesm, "identf": identf, "ibf": ibf,
        "iota8": iota8, "c8b": c8b, "ccol": ccol,
        "io128": io128, "io32": io32, "thr32": thr32, "iotac": iotac,
    }


def build_l3():
    nc = bass.Bass("TRN2", target_bir_lowering=False, debug=False)
    eof_i = nc.dram_tensor("eof", [NEXP * CAP, D], F32, kind="ExternalInput").ap()
    rtf_i = nc.dram_tensor("rtf", [128, 4, 16], F32, kind="ExternalInput").ap()
    x_i = nc.dram_tensor("xb", [L, D], F32, kind="ExternalInput").ap()
    s_i = nc.dram_tensor("sb", [L, D], F32, kind="ExternalInput").ap()
    out_o = nc.dram_tensor("out_o", [L, D], F32, kind="ExternalOutput").ap()

    es = ExitStack()
    sb = {}
    names = [("rtf", [128, 4, 16], F32), ("i1", [128, 16], I32), ("i2", [128, 16], I32)]
    for par in (0, 1):
        names += [(f"eo1_{par}", [128, D], F32), (f"eo2_{par}", [128, D], F32),
                  (f"xt_{par}", [128, D], F32), (f"st_{par}", [128, D], F32),
                  (f"t1_{par}", [128, D], F32), (f"gg_{par}", [128, D], F32),
                  (f"ot_{par}", [128, D], F32)]
    for name, shape, dt in names:
        sb[name] = es.enter_context(nc.sbuf_tensor("s_" + name, shape, dt))

    p = Prog(nc, es)

    p.emit("sp", lambda eng: eng.dma_start(out=sb["rtf"][:], in_=rtf_i),
           writes=["rtf"], dma=True, label="ld_rtf")
    p.emit("dve", lambda eng: eng.tensor_copy(sb["i1"][:], sb["rtf"][:, 0, :]),
           reads=["rtf"], writes=["i1"], label="i1")
    p.emit("dve", lambda eng: eng.tensor_copy(sb["i2"][:], sb["rtf"][:, 1, :]),
           reads=["rtf"], writes=["i2"], label="i2")

    for q in range(16):
        par = q % 2
        eo1, eo2, xt, st, t1, gg, ot = (f"eo1_{par}", f"eo2_{par}", f"xt_{par}",
                                        f"st_{par}", f"t1_{par}", f"gg_{par}",
                                        f"ot_{par}")
        p.emit("dve", lambda eng, eo1=eo1: eng.memset(sb[eo1][:], 0.0), writes=[eo1])
        p.emit("dve", lambda eng, eo2=eo2: eng.memset(sb[eo2][:], 0.0), writes=[eo2])
        p.emit("pool", lambda eng, q=q, eo1=eo1: eng.indirect_dma_start(
            out=sb[eo1][:, :], out_offset=None, in_=eof_i,
            in_offset=bass.IndirectOffsetOnAxis(sb["i1"][:, q:q + 1], 0),
            bounds_check=NEXP * CAP - 1, oob_is_err=False),
            reads=["i1", eo1], writes=[eo1], dma=True, chan=f"g1{par}", label="g1")
        p.emit("pool", lambda eng, q=q, eo2=eo2: eng.indirect_dma_start(
            out=sb[eo2][:, :], out_offset=None, in_=eof_i,
            in_offset=bass.IndirectOffsetOnAxis(sb["i2"][:, q:q + 1], 0),
            bounds_check=NEXP * CAP - 1, oob_is_err=False),
            reads=["i2", eo2], writes=[eo2], dma=True, chan=f"g2{par}", label="g2")
        p.emit("sp", lambda eng, q=q, xt=xt: eng.dma_start(
            out=sb[xt][:], in_=x_i[q * 128:(q + 1) * 128, :]),
            writes=[xt], dma=True, chan=f"x{par}", label="ld_x")
        p.emit("sp", lambda eng, q=q, st=st: eng.dma_start(
            out=sb[st][:], in_=s_i[q * 128:(q + 1) * 128, :]),
            writes=[st], dma=True, chan=f"s{par}", label="ld_s")
        p.emit("dve", lambda eng, q=q, t1=t1, eo1=eo1: eng.tensor_scalar_mul(
            sb[t1][:], sb[eo1][:], sb["rtf"][:, 2, q:q + 1]),
            reads=[eo1, "rtf"], writes=[t1], label="gmul1")
        p.emit("dve", lambda eng, q=q, gg=gg, eo2=eo2, t1=t1: eng.scalar_tensor_tensor(
            out=sb[gg][:], in0=sb[eo2][:], scalar=sb["rtf"][:, 3, q:q + 1],
            in1=sb[t1][:], op0=ALU.mult, op1=ALU.add),
            reads=[eo2, "rtf", t1], writes=[gg], label="gmul2")
        p.emit("act", lambda eng, gg=gg: eng.activation(
            out=sb[gg][:], in_=sb[gg][:], func=AF.Silu),
            reads=[gg], writes=[gg], label="silu")
        p.emit("dve", lambda eng, t1=t1, st=st, gg=gg: eng.tensor_mul(
            sb[t1][:], sb[st][:], sb[gg][:]),
            reads=[st, gg], writes=[t1], label="sg")
        p.emit("dve", lambda eng, ot=ot, xt=xt, t1=t1: eng.tensor_add(
            sb[ot][:], sb[xt][:], sb[t1][:]),
            reads=[xt, t1], writes=[ot], label="outadd")
        p.emit("sp", lambda eng, q=q, ot=ot: eng.dma_start(
            out=out_o[q * 128:(q + 1) * 128, :], in_=sb[ot][:]),
            reads=[ot], dma=True, chan=f"o{par}", label="st_o")

    with nc.Block() as block:
        p.build(block)
    es.close()
    return nc


def _l3_inputs(b, eof, rt, x_b, s_b):
    f32 = np.float32
    rt = np.asarray(rt).reshape(128, 4, 128)
    mine = rt[16 * b:16 * b + 16]
    rtf = np.ascontiguousarray(mine.transpose(2, 1, 0))
    return {
        "eof": eof.astype(f32), "rtf": rtf.astype(f32),
        "xb": x_b.astype(f32), "sb": s_b.astype(f32),
    }


def kernel(L0_attn_hidden_states, L0_attn_in_proj_w, L0_attn_conv_w,
           L0_attn_conv_b, L0_attn_x_proj_w, L0_attn_dt_proj_w, L0_attn_dt_proj_b,
           L0_attn_A_log, L0_attn_D, L0_attn_out_proj_w, L0_ffn_wg, L0_ffn_w_gate,
           L0_ffn_w_down):
    x = np.asarray(L0_attn_hidden_states)
    core_ids = list(range(NC))

    nc1 = build_l1()
    maps1 = [_l1_inputs(x[b], L0_attn_in_proj_w, L0_attn_conv_w, L0_attn_conv_b,
                        L0_attn_x_proj_w, L0_attn_dt_proj_w, L0_attn_dt_proj_b,
                        L0_attn_A_log, L0_attn_D, L0_attn_out_proj_w, L0_ffn_wg)
             for b in range(B)]
    res1 = run_bass_kernel_spmd(nc1, maps1, core_ids=core_ids).results

    lg_full = np.concatenate([res1[b]["lgT_o"].T for b in range(B)], axis=0)
    xn_full = np.concatenate(
        [np.ascontiguousarray(
            np.asarray(res1[b]["xnT_o"]).reshape(NCH, 128, 8, TC)
            .transpose(0, 3, 2, 1).reshape(L, D))
         for b in range(B)], axis=0)
    s_full = [res1[b]["s_o"] for b in range(B)]

    nc2 = build_l2()
    maps2 = [_l2_inputs(e, lg_full, xn_full, np.asarray(L0_ffn_w_gate)[e],
                        np.asarray(L0_ffn_w_down)[e]) for e in range(NEXP)]
    res2 = run_bass_kernel_spmd(nc2, maps2, core_ids=core_ids).results

    eof = np.concatenate([res2[e]["eo_o"] for e in range(NEXP)], axis=0)
    rt = res2[0]["rt_o"]

    nc3 = build_l3()
    maps3 = [_l3_inputs(b, eof, rt, x[b], s_full[b]) for b in range(B)]
    res3 = run_bass_kernel_spmd(nc3, maps3, core_ids=core_ids).results

    return np.stack([res3[b]["out_o"] for b in range(B)], axis=0)
